# version 02
# baseline (speedup 1.0000x reference)
import numpy as np
from contextlib import ExitStack

import concourse.bass as bass
import concourse.mybir as mybir
from concourse.bass_utils import run_bass_kernel_spmd

F32 = mybir.dt.float32
BF16 = mybir.dt.bfloat16
I32 = mybir.dt.int32
AF = mybir.ActivationFunctionType
OP = mybir.AluOpType

D = 2048
HID = 5632
NCT = D // 128
NHT = HID // 128
W = 1026
TPC = 1024
CH = ((2, 514), (514, 1026))
LN2CH = ((0, 512), (512, 1024), (1024, 1026))
EPS = 1e-5
WKV_EPS = 1e-8

CC = {}
_c = 0
for _n, _w in (("mu_r", 16), ("mu_k", 16), ("mu_v", 16), ("cmu_r", 16),
               ("cmu_k", 16), ("g1", 16), ("b1", 16), ("g2", 16), ("b2", 16),
               ("u", 16), ("s_down", 16), ("s_gate", 44), ("s_up", 44),
               ("hmask", 1)):
    CC[_n] = (_c, _c + _w)
    _c += _w
NCON = _c


class _CutDone(Exception):
    pass


def _build(cut=None):
    nc = bass.Bass("TRN2", target_bir_lowering=False,
                   detect_race_conditions=False)
    xs = nc.dram_tensor("xs", [W, D], F32, kind="ExternalInput").ap()
    consts = nc.dram_tensor("consts", [128, NCON], F32, kind="ExternalInput").ap()
    wk = nc.dram_tensor("wk", [NCT, 128, D], F32, kind="ExternalInput").ap()
    wv = nc.dram_tensor("wv", [NCT, 128, D], F32, kind="ExternalInput").ap()
    wr = nc.dram_tensor("wr", [NCT, 128, D], F32, kind="ExternalInput").ap()
    wo = nc.dram_tensor("wo", [NCT, 128, D], F32, kind="ExternalInput").ap()
    wr2 = nc.dram_tensor("wr2", [NCT, 128, D], F32, kind="ExternalInput").ap()
    wk2 = nc.dram_tensor("wk2", [NCT, 128, D], F32, kind="ExternalInput").ap()
    wv2 = nc.dram_tensor("wv2", [NCT, 128, D], F32, kind="ExternalInput").ap()
    wg = nc.dram_tensor("wg", [NHT, 128, D], I32, kind="ExternalInput").ap()
    wu = nc.dram_tensor("wu", [NHT, 128, D], I32, kind="ExternalInput").ap()
    wd = nc.dram_tensor("wd", [NCT, 128, HID], I32, kind="ExternalInput").ap()
    zsave = nc.dram_tensor("zsave", [128, NCT * W], F32, kind="ExternalOutput").ap()
    y = nc.dram_tensor("y", [TPC, D], F32, kind="ExternalOutput").ap()
    dbg = (nc.dram_tensor("dbg", [128, 2 * NCT * W], F32,
                          kind="ExternalOutput").ap() if cut is not None else None)

    es = ExitStack()
    S = lambda *a, **k: es.enter_context(nc.sbuf_tensor(*a, **k))
    P = lambda *a, **k: es.enter_context(nc.psum_tensor(*a, **k))

    R1 = S("R1", [128, NCT * W], F32)
    R2 = S("R2", [128, NCT * W], F32)
    B3 = S("B3", [128, NCT * W], BF16)
    B4 = S("B4", [128, NCT * W], BF16)
    SLAB = S("SLAB", [128, 2 * D], BF16)
    ctile = S("ctile", [128, NCON], F32)
    dmu = S("dmu", [128, 5 * 16], F32)
    epsA = S("epsA", [128, 1], F32)
    nhA = S("nhA", [128, 1], F32)
    wepsA = S("wepsA", [128, 1], F32)
    identf = S("identf", [128, 128], F32)
    identb = S("identb", [128, 128], BF16)
    onesb = S("onesb", [128, 128], BF16)
    scrFA = S("scrFA", [128, 512], F32)
    scrGA = S("scrGA", [128, 512], BF16)
    stKU = S("stKU", [128, 16], F32)
    stE = S("stE", [128, 16], F32)
    stG = S("stG", [128, 16], BF16)
    stats = S("stats", [128, 24], F32)

    R1z = R1.ap().rearrange("p (c t) -> p c t", t=W)
    R1bf = R1.ap().bitcast(BF16)
    R1a = R1bf[:, 0:NCT * W].rearrange("p (c t) -> p c t", t=W)
    R1acc = R1bf[:, NCT * W:2 * NCT * W].rearrange("p (c t) -> p c t", t=W)
    R1x = R1.ap()[:, 0:8 * 2048].rearrange("p (k t) -> p k t", t=2048)
    R1m = R1bf[:, 0:32 * TPC].rearrange("p (h t) -> p h t", t=TPC)
    m2b = R1bf[:, 2 * NCT * W - 2 * W:2 * NCT * W - W]
    istd2b = R1bf[:, 2 * NCT * W - W:2 * NCT * W]
    R2z = R2.ap().rearrange("p (c t) -> p c t", t=W)
    R2bf = R2.ap().bitcast(BF16)
    R2a = R2bf[:, 0:NCT * W].rearrange("p (c t) -> p c t", t=W)
    R2b = R2bf[:, NCT * W:2 * NCT * W].rearrange("p (c t) -> p c t", t=W)
    B3h = B3.ap().rearrange("p (c t) -> p c t", t=W)
    B3ds = B3.ap().rearrange("p (s h) -> p s h", s=2)[:, :, 0:HID]
    B4h = B4.ap().rearrange("p (c t) -> p c t", t=W)
    B4f = B4.ap().bitcast(F32)
    xstage = [B4f[:, 0:2048], B4f[:, 2048:4096]]
    zzst = B4.ap()[:, 8192:8192 + 2048]
    dscr = B4.ap()[:, 10240:10240 + 2048]
    vdmp = B4.ap()[:, 12288:12288 + 2048]
    B4m = B4.ap()[:, 0:12 * TPC].rearrange("p (h t) -> p h t", t=TPC)
    ostage = [B4f[:, 0:2048], B4f[:, 2048:4096]]
    slab = SLAB.ap().rearrange("p (s d) -> p s d", s=2)
    ffn_bases = [slab[:, 0, :], slab[:, 1, :],
                 B4.ap()[:, 12288:14336], B4.ap()[:, 14336:16384]]
    scrH = scrFA.ap()

    def cvec(name, i=0):
        a, b = CC[name]
        return ctile[:, a + i:a + i + 1]

    def crange(name):
        a, b = CC[name]
        return ctile[:, a:b]

    pb = [P(f"pb{i}", [128, 512], F32) for i in range(6)]
    pz = [P(f"pz{i}", [128, 1024], BF16) for i in range(2)]

    prog = {k: [] for k in ("sync", "gpsimd", "tensor", "vector", "scalar")}
    sems = {}
    cnt = {}
    for sname in ("qs0", "qs1", "qg0", "qg1", "qg2", "qg3", "qc", "te", "ve", "se", "ge"):
        sems[sname] = es.enter_context(nc.semaphore(sname))
        cnt[sname] = 0

    last_ins = {k: None for k in prog}

    def emit(eng, fn):
        prog[eng].append(fn)

    def W8(eng, sem, val):
        emit(eng, lambda e, s=sems[sem], v=val: e.wait_ge(s, v))

    def RUN(eng, meth, *args, inc=None, **kw):
        if inc is not None:
            cnt[inc] += 16 if inc.startswith("q") else 1
        v = cnt[inc] if inc is not None else None
        def f(e, meth=meth, args=args, kw=kw, inc=inc, eng=eng):
            ins = getattr(e, meth)(*args, **kw)
            last_ins[eng] = ins
            if inc is not None:
                ins.then_inc(sems[inc], 16 if inc.startswith("q") else 1)
        emit(eng, f)
        return v

    def GINC(eng, sem):
        cnt[sem] += 1
        emit(eng, lambda e, s=sems[sem], eng=eng: last_ins[eng].then_inc(s, 1))
        return cnt[sem]

    def CUT(n):
        if cut is not None and cut == n:
            raise _CutDone()

    def DMA(eng, sem, out, in_, **kw):
        W8(eng, sem, cnt[sem])
        return RUN(eng, "dma_start", inc=sem, out=out, in_=in_, **kw)

    try:
        v_consts = DMA("gpsimd", "qc", ctile[:], consts)
        RUN("gpsimd", "memset", identf[:], 0.0)
        g_ident = RUN("gpsimd", "affine_select", inc="ge", out=identf[:],
                      in_=identf[:], compare_op=OP.not_equal, fill=1.0, base=0,
                      pattern=[[-1, 128]], channel_multiplier=1)

        W8("vector", "ge", g_ident)
        RUN("vector", "tensor_copy", identb[:], identf[:])
        emit("vector", lambda e: e.memset(onesb[:], 1.0))
        emit("vector", lambda e: e.memset(epsA[:], EPS))
        emit("vector", lambda e: e.memset(nhA[:], -0.5))
        emit("vector", lambda e: e.memset(wepsA[:], WKV_EPS))
        W8("vector", "qc", v_consts)
        for i, nm in enumerate(("mu_r", "mu_k", "mu_v", "cmu_r", "cmu_k")):
            RUN("vector", "tensor_scalar", dmu[:, 16 * i:16 * (i + 1)], crange(nm),
                1.0, None, OP.subtract)
        v_setup = GINC("vector", "ve")

        slab_state = {"idx": 0, "consumed": []}

        def load_slab(src_ap):
            i = slab_state["idx"]
            slab_state["idx"] += 1
            slot = i % 2
            qsem = "qg%d" % slot
            if i >= 2:
                W8("gpsimd", "te", slab_state["consumed"][i - 2])
            v = DMA("gpsimd", qsem, slab[:, slot, :], src_ap)
            return slab[:, slot, :], qsem, v

        ffn_state = {"pend": {}, "consumed": {}, "first": {}}

        def ffn_issue(i, src_ap):
            slot = i % 4
            qsem = "qg%d" % slot
            if i >= 4:
                W8("gpsimd", "te", ffn_state["consumed"][i - 4])
            elif slot in ffn_state["first"]:
                W8("gpsimd", "te", ffn_state["first"][slot])
            v = DMA("gpsimd", qsem, ffn_bases[slot], src_ap)
            ffn_state["pend"][i] = (slot, qsem, v)

        def ffn_finish(i):
            slot, qsem, v = ffn_state["pend"].pop(i)
            W8("gpsimd", qsem, v)
            g = RUN("gpsimd", "tensor_scalar", ffn_bases[slot], ffn_bases[slot],
                    128.0, None, OP.subtract, inc="ge")
            return ffn_bases[slot], g

        def slab_consumed():
            slab_state["consumed"].append(cnt["te"])

        dslab_state = {"idx": 0, "consumed": [], "pend": {}}

        def load_dslab(src_ap, first_wait=None):
            i = dslab_state["idx"]
            dslab_state["idx"] += 1
            slot = i % 2
            qsem = "qg%d" % slot
            if i >= 2:
                W8("gpsimd", "te", dslab_state["consumed"][i - 2])
            elif first_wait is not None:
                W8("gpsimd", "te", first_wait)
            v = DMA("gpsimd", qsem, B3ds[:, slot, 0:HID], src_ap)
            dslab_state["pend"][i] = (slot, qsem, v)
            return i

        def dslab_finish(i):
            slot, qsem, v = dslab_state["pend"].pop(i)
            W8("gpsimd", qsem, v)
            g = RUN("gpsimd", "tensor_scalar", B3ds[:, slot, 0:HID],
                    B3ds[:, slot, 0:HID], 128.0, None, OP.subtract, inc="ge")
            return B3ds[:, slot, 0:HID], g

        tiles = [(k, 2 + 128 * k, 128) for k in range(8)] + [(8, 0, 2)]
        tp_done = [None] * 9
        xst_v = [None] * 9
        vvar_hist = [None] * 9
        prev_pair = {}
        for k, row0, nrow in tiles:
            buf = k % 2
            st0 = 12 * (k % 2)
            if k >= 2:
                W8("sync", "te", tp_done[k - 2])
            xqs = "qs%d" % buf
            xst_v[k] = DMA("sync", xqs, xstage[buf][0:nrow, :],
                           xs[row0:row0 + nrow, :])
            xv = xstage[buf][0:nrow, :]
            zzv = zzst[0:nrow, :]
            su = stats[:, st0 + 0:st0 + 1][0:nrow, :]
            sq = stats[:, st0 + 1:st0 + 2][0:nrow, :]
            sm = stats[:, st0 + 2:st0 + 3][0:nrow, :]
            smm = stats[:, st0 + 3:st0 + 4][0:nrow, :]
            svar = stats[:, st0 + 4:st0 + 5][0:nrow, :]
            sii = stats[:, st0 + 5:st0 + 6][0:nrow, :]
            svz = stats[:, st0 + 6:st0 + 7][0:nrow, :]
            sistd = stats[:, st0 + 7:st0 + 8][0:nrow, :]
            sistd2 = stats[:, st0 + 8:st0 + 9][0:nrow, :]
            slnt = stats[:, st0 + 9:st0 + 10][0:nrow, :]
            W8("vector", xqs, xst_v[k])
            RUN("vector", "scalar_tensor_tensor", vdmp[0:nrow, :], xv, 1.0, xv,
                OP.bypass, OP.mult)
            RUN("vector", "reduce_sum", su, xv, mybir.AxisListType.X, inc="ve")
            v_t = RUN("vector", "reduce_sum", sq, vdmp[0:nrow, :],
                      mybir.AxisListType.X, inc="ve")
            W8("vector", "ve", v_t)
            v_t = RUN("vector", "tensor_scalar", sm, su, 1.0 / D, None, OP.mult,
                      inc="ve")
            W8("vector", "ve", v_t)
            v_t = RUN("vector", "tensor_tensor", smm, sm, sm, OP.mult, inc="ve")
            W8("vector", "ve", v_t)
            v_var = RUN("vector", "scalar_tensor_tensor", svar, sq, 1.0 / D, smm,
                        OP.mult, OP.subtract, inc="ve")
            W8("vector", "ve", v_var)
            vvar_hist[k] = v_var
            W8("scalar", "ve", v_var)
            se_t = RUN("scalar", "activation", slnt, svar, AF.Ln,
                       bias=epsA[0:nrow, :], inc="se")
            W8("scalar", "se", se_t)
            se_istd = RUN("scalar", "activation", sistd, slnt, AF.Exp,
                          scale=nhA[0:nrow, :], inc="se")
            W8("vector", "se", se_istd)
            v_t = RUN("vector", "tensor_tensor", sii, sistd, sistd, OP.mult,
                      inc="ve")
            W8("vector", "ve", v_t)
            v_vz = RUN("vector", "tensor_tensor", svz, svar, sii, OP.mult,
                       inc="ve")
            W8("scalar", "ve", v_vz)
            se_t = RUN("scalar", "activation", slnt, svz, AF.Ln,
                       bias=epsA[0:nrow, :], inc="se")
            W8("scalar", "se", se_t)
            se_istd2 = RUN("scalar", "activation", sistd2, slnt, AF.Exp,
                           scale=nhA[0:nrow, :], inc="se")
            RUN("vector", "tensor_scalar", xv, xv, sm, sistd, OP.subtract, OP.mult)
            W8("vector", "se", se_istd2)
            if k >= 1:
                W8("vector", "te", tp_done[k - 1])
            v_zz = RUN("vector", "tensor_scalar", zzv, xv, sistd2, None, OP.mult,
                       inc="ve")
            W8("tensor", "ve", v_zz)
            if k == 0:
                W8("tensor", "ge", g_ident)
                W8("tensor", "ve", v_setup)
            col0 = row0
            for g in range(4):
                bank = pb[g % 2]
                key = ("p0z", g % 2)
                if key in prev_pair:
                    W8("tensor", "ve", prev_pair[key])
                for j in range(4):
                    ci = 4 * g + j
                    RUN("tensor", "transpose", bank[0:128, j * 128:j * 128 + nrow],
                        xv[:, ci * 128:(ci + 1) * 128], identf[0:nrow, 0:nrow])
                t_g = GINC("tensor", "te")
                W8("vector", "te", t_g)
                ev = RUN("vector", "tensor_copy",
                         R1z[:, 4 * g:4 * g + 4, col0:col0 + nrow],
                         bank[:].rearrange("p (c t) -> p c t", c=4)[:, :, 0:nrow],
                         inc="ve")
                prev_pair[key] = ev
            for g in range(2):
                bank = pz[g]
                key = ("p0zz", g)
                if key in prev_pair:
                    W8("tensor", "se", prev_pair[key])
                for j in range(8):
                    ci = 8 * g + j
                    RUN("tensor", "transpose", bank[0:128, j * 128:j * 128 + nrow],
                        zzv[:, ci * 128:(ci + 1) * 128], identb[0:nrow, 0:nrow])
                t_g = GINC("tensor", "te")
                W8("scalar", "te", t_g)
                ev = RUN("scalar", "activation",
                         B3h[:, 8 * g:8 * g + 8, col0:col0 + nrow],
                         bank[:].rearrange("p (c t) -> p c t", c=8)[:, :, 0:nrow],
                         AF.Copy, inc="se")
                prev_pair[key] = ev
            tp_done[k] = cnt["te"]

        W8("vector", "se", cnt["se"])
        for ci in range(NCT):
            RUN("vector", "tensor_scalar", B3h[:, ci, :], B3h[:, ci, :],
                cvec("g1", ci), cvec("b1", ci), OP.mult, OP.add)
        for ci in range(NCT):
            RUN("vector", "tensor_scalar", B3h[:, ci, 0:2], B3h[:, ci, 0:2],
                cvec("hmask"), None, OP.mult)
        v_h = GINC("vector", "ve")

        CUT(1)
        W8("sync", "ve", v_h)
        qs_zoff = DMA("sync", "qs0", zsave, R1.ap())

        for ci in range(NCT):
            RUN("vector", "tensor_tensor", B4h[:, ci, 1:W], B3h[:, ci, 1:W],
                B3h[:, ci, 0:W - 1], OP.subtract)
        for ci in range(NCT):
            RUN("vector", "scalar_tensor_tensor", R2a[:, ci, 1:W], B4h[:, ci, 1:W],
                dmu[:, 16 * 1 + ci:16 * 1 + ci + 1], B3h[:, ci, 1:W], OP.mult, OP.add)
        v_xk = GINC("vector", "ve")
        for ci in range(NCT):
            RUN("vector", "scalar_tensor_tensor", R2b[:, ci, 1:W], B4h[:, ci, 1:W],
                dmu[:, 16 * 2 + ci:16 * 2 + ci + 1], B3h[:, ci, 1:W], OP.mult, OP.add)
        v_xv = GINC("vector", "ve")
        W8("vector", "qs0", qs_zoff)
        for ci in range(NCT):
            RUN("vector", "scalar_tensor_tensor", R1a[:, ci, 1:W], B4h[:, ci, 1:W],
                dmu[:, 16 * 0 + ci:16 * 0 + ci + 1], B3h[:, ci, 1:W], OP.mult, OP.add)
        v_xr = GINC("vector", "ve")

        mm_col = ((2, 514, 0), (514, 1026, 1))

        def mm_matrix(stat_base, mov3, banks, stray_col, first_waits=()):
            for wsem, wval in first_waits:
                W8("tensor", wsem, wval)
            for ci in range(NCT):
                st = stat_base[:, ci * 128:(ci + 1) * 128]
                for c0, c1, bo in mm_col:
                    RUN("tensor", "matmul", banks[bo][:, 0:c1 - c0], st,
                        mov3[:, ci, c0:c1], start=(ci == 0), stop=(ci == NCT - 1),
                        skip_group_check=True)
                if stray_col is not None:
                    RUN("tensor", "matmul", pb[4][:, stray_col:stray_col + 1], st,
                        mov3[:, ci, 1:2], start=(ci == 0), stop=(ci == NCT - 1),
                        skip_group_check=True)
            return GINC("tensor", "te")

        kv_free = {}
        te_kv_last = None
        u_all = crange("u")
        for oj in range(16):
            base_k, qsk, qk = load_slab(wk[oj])
            base_v, qsv, qv = load_slab(wv[oj])
            fw = [(qsk, qk), ("ve", v_xk)]
            if oj > 0:
                fw.append(("ve", kv_free[("k", oj - 1)]))
            te_k = mm_matrix(base_k, R2a, (pb[0], pb[1]), oj, fw)
            slab_consumed()
            fw = [(qsv, qv), ("ve", v_xv)]
            if oj > 0:
                fw.append(("ve", kv_free[("v", oj - 1)]))
            te_v = mm_matrix(base_v, R2b, (pb[2], pb[3]), 16 + oj, fw)
            slab_consumed()
            te_kv_last = te_v
            u_oj = cvec("u", oj)
            W8("scalar", "te", te_k)
            last_s6 = None
            for c in range(2):
                c0, c1, _ = mm_col[c]
                if last_s6 is not None:
                    W8("scalar", "ve", last_s6)
                s1 = RUN("scalar", "activation", scrFA[:], pb[c][:], AF.Exp,
                         bias=u_oj, inc="se")
                W8("vector", "se", s1)
                v2 = RUN("vector", "tensor_scalar", scrFA[:], scrFA[:], WKV_EPS,
                         None, OP.add, inc="ve")
                W8("scalar", "ve", v2)
                s3 = RUN("scalar", "activation", scrFA[:], scrFA[:], AF.Ln,
                         inc="se")
                W8("vector", "se", s3)
                v4 = RUN("vector", "tensor_tensor", scrFA[:], pb[c][:], scrFA[:],
                         OP.subtract, inc="ve")
                if c == 1:
                    kv_free[("k", oj)] = v4
                W8("scalar", "ve", v4)
                if c == 0:
                    W8("scalar", "te", te_v)
                s5 = RUN("scalar", "activation", scrGA[:], scrFA[:], AF.Exp,
                         bias=u_oj, inc="se")
                W8("vector", "se", s5)
                last_s6 = RUN("vector", "tensor_tensor", R1acc[:, oj, c0:c1],
                              scrGA[:], pb[2 + c][:], OP.mult, inc="ve")
            kv_free[("v", oj)] = last_s6

        CUT(2)
        W8("vector", "te", te_kv_last)
        W8("vector", "ve", kv_free[("v", 15)])
        RUN("vector", "tensor_tensor", stKU[:], pb[4][:, 0:16], u_all, OP.add)
        v_sku = GINC("vector", "ve")
        W8("scalar", "ve", v_sku)
        se_a = RUN("scalar", "activation", stE[:], stKU[:], AF.Exp, inc="se")
        W8("vector", "se", se_a)
        v_b = RUN("vector", "tensor_scalar", stE[:], stE[:], WKV_EPS, None, OP.add,
                  inc="ve")
        W8("scalar", "ve", v_b)
        se_st = RUN("scalar", "activation", stE[:], stE[:], AF.Ln, inc="se")
        W8("vector", "se", se_st)
        RUN("vector", "tensor_tensor", stKU[:], stKU[:], stE[:], OP.subtract)
        v_st = GINC("vector", "ve")
        W8("scalar", "ve", v_st)
        se_st = RUN("scalar", "activation", stG[:], stKU[:], AF.Exp, inc="se")
        W8("vector", "se", se_st)
        v_stray_kv = RUN("vector", "tensor_tensor", R1acc[:, :, 1], stG[:],
                         pb[4][:, 16:32], OP.mult, inc="ve")

        r_free = {}
        te_r_last = None
        for oj in range(16):
            base_r, qsr, qr = load_slab(wr[oj])
            par = oj % 2
            banks = (pb[0], pb[1]) if par == 0 else (pb[2], pb[3])
            fw = [(qsr, qr), ("ve", v_xr)]
            if oj == 0:
                fw.append(("ve", v_stray_kv))
            if oj >= 2:
                fw.append(("ve", r_free[oj - 2]))
            te_r = mm_matrix(base_r, R1a, banks, 32 + oj, fw)
            slab_consumed()
            te_r_last = te_r
            W8("scalar", "te", te_r)
            last_v = r_free.get(oj - 1)
            for c in range(2):
                c0, c1, _ = mm_col[c]
                if last_v is not None:
                    W8("scalar", "ve", last_v)
                s_sg = RUN("scalar", "activation", scrGA[:], banks[c][:],
                           AF.Sigmoid, inc="se")
                W8("vector", "se", s_sg)
                last_v = RUN("vector", "tensor_tensor", R1acc[:, oj, c0:c1],
                             R1acc[:, oj, c0:c1], scrGA[:], OP.mult, inc="ve")
            r_free[oj] = last_v
        W8("scalar", "te", te_r_last)
        W8("scalar", "ve", r_free[15])
        se_st = RUN("scalar", "activation", stG[:], pb[4][:, 32:48], AF.Sigmoid,
                    inc="se")
        W8("vector", "se", se_st)
        v_stray_r = RUN("vector", "tensor_tensor", R1acc[:, :, 1], R1acc[:, :, 1],
                        stG[:], OP.mult, inc="ve")
        v_acc_done = v_stray_r

        CUT(3)
        W8("sync", "qs0", qs_zoff)
        W8("sync", "te", te_kv_last)
        qs_zrel = DMA("sync", "qs1", R2.ap(), zsave)

        wo_free = {}
        te_wo_last = None
        for oj in range(16):
            base_o, qso, qo = load_slab(wo[oj])
            par = oj % 2
            banks = (pb[0], pb[1]) if par == 0 else (pb[2], pb[3])
            fw = [(qso, qo), ("ve", v_acc_done)]
            if oj == 0:
                fw.append(("ve", v_stray_r))
            if oj >= 2:
                fw.append(("ve", wo_free[oj - 2]))
            te_wo = mm_matrix(base_o, R1acc, banks, 48 + oj, fw)
            slab_consumed()
            te_wo_last = te_wo
            W8("vector", "te", te_wo)
            if oj == 0:
                W8("vector", "qs1", qs_zrel)
            for c in range(2):
                c0, c1, _ = mm_col[c]
                RUN("vector", "tensor_tensor", R2z[:, oj, c0:c1], R2z[:, oj, c0:c1],
                    banks[c][:], OP.add)
            wo_free[oj] = GINC("vector", "ve")
        W8("vector", "te", te_wo_last)
        v_z2 = RUN("vector", "tensor_tensor", R2z[:, :, 1], R2z[:, :, 1],
                   pb[4][:, 48:64], OP.add, inc="ve")

        CUT(4)
        W8("vector", "ve", v_z2)
        for ci in range(NCT):
            RUN("vector", "tensor_scalar", B3h[:, ci, :], R2z[:, ci, :], 1.0, None,
                OP.mult)
        for ci in range(NCT):
            RUN("vector", "tensor_tensor", B4h[:, ci, :], B3h[:, ci, :],
                B3h[:, ci, :], OP.mult)
        v_sq = GINC("vector", "ve")
        W8("tensor", "ve", v_sq)
        W8("tensor", "ve", wo_free[15])
        ln2banks = []
        for idx, (c0, c1) in enumerate(LN2CH):
            bsum = pb[idx][:, 0:c1 - c0] if idx < 2 else pb[4][:, 100:102]
            bsq = pb[2 + idx][:, 0:c1 - c0] if idx < 2 else pb[4][:, 102:104]
            for ci in range(NCT):
                RUN("tensor", "matmul", bsum, onesb[:], B3h[:, ci, c0:c1],
                    start=(ci == 0), stop=(ci == NCT - 1), skip_group_check=True)
            for ci in range(NCT):
                RUN("tensor", "matmul", bsq, onesb[:], B4h[:, ci, c0:c1],
                    start=(ci == 0), stop=(ci == NCT - 1), skip_group_check=True)
            ln2banks.append((bsum, bsq))
        te_ln2 = GINC("tensor", "te")

        W8("vector", "te", te_ln2)
        W8("scalar", "te", te_ln2)
        for idx, (c0, c1) in enumerate(LN2CH):
            bsum, bsq = ln2banks[idx]
            n = c1 - c0
            RUN("vector", "tensor_scalar", m2b[:, c0:c1], bsum, 1.0 / D, None,
                OP.mult)
            RUN("vector", "tensor_tensor", scrH[:, 0:n], m2b[:, c0:c1],
                m2b[:, c0:c1], OP.mult)
            v_t = RUN("vector", "scalar_tensor_tensor", scrH[:, 0:n], bsq, 1.0 / D,
                      scrH[:, 0:n], OP.mult, OP.subtract, inc="ve")
            W8("scalar", "ve", v_t)
            RUN("scalar", "activation", scrH[:, 0:n], scrH[:, 0:n], AF.Ln,
                bias=epsA[:])
            se_i = RUN("scalar", "activation", istd2b[:, c0:c1], scrH[:, 0:n],
                       AF.Exp, scale=nhA[:], inc="se")
            W8("vector", "se", se_i)
            se_ln2 = se_i
        W8("vector", "se", se_ln2)
        for ci in range(NCT):
            RUN("vector", "tensor_tensor", B3h[:, ci, 1:W], R2z[:, ci, 1:W],
                m2b[:, 1:W], OP.subtract)
            RUN("vector", "tensor_tensor", B3h[:, ci, 1:W], B3h[:, ci, 1:W],
                istd2b[:, 1:W], OP.mult)
            RUN("vector", "tensor_scalar", B3h[:, ci, 1:W], B3h[:, ci, 1:W],
                cvec("g2", ci), cvec("b2", ci), OP.mult, OP.add)
            RUN("vector", "tensor_scalar", B3h[:, ci, 1:2], B3h[:, ci, 1:2],
                cvec("hmask"), None, OP.mult)
        v_h2 = GINC("vector", "ve")

        for ci in range(NCT):
            RUN("vector", "tensor_tensor", B4h[:, ci, 2:W], B3h[:, ci, 2:W],
                B3h[:, ci, 1:W - 1], OP.subtract)
        for ci in range(NCT):
            RUN("vector", "scalar_tensor_tensor", R1a[:, ci, 2:W], B4h[:, ci, 2:W],
                dmu[:, 16 * 3 + ci:16 * 3 + ci + 1], B3h[:, ci, 2:W], OP.mult, OP.add)
        v_xr2 = GINC("vector", "ve")
        for ci in range(NCT):
            RUN("vector", "scalar_tensor_tensor", R1acc[:, ci, 2:W], B4h[:, ci, 2:W],
                dmu[:, 16 * 4 + ci:16 * 4 + ci + 1], B3h[:, ci, 2:W], OP.mult, OP.add)
        v_xk2 = GINC("vector", "ve")

        def mm_matrix2(stat_base, mov3, banks, first_waits=(), dslab=False):
            for wsem, wval in first_waits:
                W8("tensor", wsem, wval)
            nk = NHT if dslab else NCT
            for ci in range(nk):
                st = stat_base[:, ci * 128:(ci + 1) * 128]
                for c0, c1, bo in mm_col:
                    if dslab:
                        mov = mov3(ci, c0, c1)
                    else:
                        mov = mov3[:, ci, c0:c1]
                    RUN("tensor", "matmul", banks[bo][:, 0:c1 - c0], st, mov,
                        start=(ci == 0), stop=(ci == nk - 1),
                        skip_group_check=True)
            return GINC("tensor", "te")

        CUT(5)
        r2_free = {}
        for oj in range(16):
            base_q, qsq, q = load_slab(wr2[oj])
            par = oj % 2
            banks = (pb[0], pb[1]) if par == 0 else (pb[2], pb[3])
            fw = [(qsq, q), ("ve", v_xk2)]
            if oj >= 2:
                fw.append(("se", r2_free[oj - 2]))
            te_g = mm_matrix2(base_q, R1a, banks, fw)
            slab_consumed()
            W8("scalar", "te", te_g)
            for c in range(2):
                c0, c1, _ = mm_col[c]
                RUN("scalar", "activation", B3h[:, oj, c0:c1], banks[c][:],
                    AF.Sigmoid)
            r2_free[oj] = GINC("scalar", "se")
        k2_free = {}
        te_wk2_last = None
        for oj in range(16):
            base_q, qsq, q = load_slab(wk2[oj])
            par = oj % 2
            banks = (pb[0], pb[1]) if par == 0 else (pb[2], pb[3])
            fw = [(qsq, q), ("ve", v_xk2)]
            if oj == 0:
                fw.append(("se", r2_free[15]))
            if oj >= 2:
                fw.append(("ve", k2_free[oj - 2]))
            te_g = mm_matrix2(base_q, R1acc, banks, fw)
            slab_consumed()
            te_wk2_last = te_g
            W8("scalar", "te", te_g)
            last_v = k2_free.get(oj - 1)
            for c in range(2):
                c0, c1, _ = mm_col[c]
                if last_v is not None:
                    W8("scalar", "ve", last_v)
                s_g = RUN("scalar", "activation", scrGA[:], banks[c][:], AF.Relu,
                          inc="se")
                W8("vector", "se", s_g)
                last_v = RUN("vector", "tensor_tensor", B4h[:, oj, c0:c1],
                             scrGA[:], scrGA[:], OP.mult, inc="ve")
            k2_free[oj] = last_v

        W8("sync", "te", te_wk2_last)
        qs_x = {}
        for k in range(8):
            qs_x["qs%d" % (k % 2)] = DMA("sync", "qs%d" % (k % 2), R1x[:, k, :],
                                         xs[2 + 128 * k:2 + 128 * (k + 1), :])

        CUT(6)
        v2_free = {}
        xadd_free = {}
        for oj in range(16):
            base_q, qsq, q = load_slab(wv2[oj])
            par = oj % 2
            banks = (pb[0], pb[1]) if par == 0 else (pb[2], pb[3])
            fw = [(qsq, q), ("ve", k2_free[15])]
            if oj >= 2:
                fw.append(("ve", v2_free[oj - 2]))
            te_g = mm_matrix2(base_q, B4h, banks, fw)
            slab_consumed()
            W8("vector", "te", te_g)
            for c in range(2):
                c0, c1, _ = mm_col[c]
                RUN("vector", "tensor_tensor", scrH[:], B3h[:, oj, c0:c1],
                    banks[c][:], OP.mult)
                RUN("vector", "tensor_tensor", R2z[:, oj, c0:c1], R2z[:, oj, c0:c1],
                    scrH[:], OP.add)
            v2_free[oj] = GINC("vector", "ve")
            ci = oj
            W8("tensor", "ve", v2_free[oj])
            if oj == 0:
                for _qn, _qv in qs_x.items():
                    W8("tensor", _qn, _qv)
            else:
                W8("tensor", "ve", xadd_free[oj - 1])
            for kk in range(8):
                bank = pb[4] if kk < 4 else pb[5]
                RUN("tensor", "transpose", bank[:, (kk % 4) * 128:(kk % 4 + 1) * 128],
                    R1x[:, kk, ci * 128:(ci + 1) * 128], identf[:])
            te_tp = GINC("tensor", "te")
            W8("vector", "te", te_tp)
            for half in range(2):
                RUN("vector", "tensor_tensor",
                    R2z[:, ci, 2 + 512 * half:2 + 512 * (half + 1)],
                    R2z[:, ci, 2 + 512 * half:2 + 512 * (half + 1)],
                    pb[4 + half][:], OP.add)
            xadd_free[oj] = GINC("vector", "ve")
        v_x1 = xadd_free[15]

        CUT(7)
        W8("vector", "ve", v_x1)
        for ci in range(NCT):
            RUN("vector", "tensor_scalar", B3h[:, ci, 2:W], R2z[:, ci, 2:W], 1.0,
                None, OP.mult)
        for ci in range(NCT):
            RUN("vector", "tensor_tensor", B4h[:, ci, 2:W], B3h[:, ci, 2:W],
                B3h[:, ci, 2:W], OP.mult)
        v_x1sq = GINC("vector", "ve")
        W8("tensor", "ve", v_x1sq)
        ffnbanks = []
        for idx, (c0, c1) in enumerate(CH):
            bsum = pb[idx]
            bsq = pb[2 + idx]
            for ci in range(NCT):
                RUN("tensor", "matmul", bsum[:], onesb[:], B3h[:, ci, c0:c1],
                    start=(ci == 0), stop=(ci == NCT - 1), skip_group_check=True)
            for ci in range(NCT):
                RUN("tensor", "matmul", bsq[:], onesb[:], B4h[:, ci, c0:c1],
                    start=(ci == 0), stop=(ci == NCT - 1), skip_group_check=True)
            ffnbanks.append((bsum, bsq))
        te_ffnln = GINC("tensor", "te")
        W8("vector", "te", te_ffnln)
        W8("scalar", "te", te_ffnln)
        for idx, (c0, c1) in enumerate(CH):
            bsum, bsq = ffnbanks[idx]
            RUN("vector", "tensor_scalar", m2b[:, c0:c1], bsum[:], 1.0 / D, None,
                OP.mult)
            RUN("vector", "tensor_tensor", scrH[:], m2b[:, c0:c1], m2b[:, c0:c1],
                OP.mult)
            v_t = RUN("vector", "scalar_tensor_tensor", scrH[:], bsq[:], 1.0 / D,
                      scrH[:], OP.mult, OP.subtract, inc="ve")
            W8("scalar", "ve", v_t)
            RUN("scalar", "activation", scrH[:], scrH[:], AF.Ln, bias=epsA[:])
            se_i = RUN("scalar", "activation", istd2b[:, c0:c1], scrH[:], AF.Exp,
                       scale=nhA[:], inc="se")
            W8("vector", "se", se_i)
            se_ffnln = se_i
        W8("vector", "se", se_ffnln)
        for ci in range(NCT):
            RUN("vector", "tensor_tensor", B3h[:, ci, 2:W], R2z[:, ci, 2:W],
                m2b[:, 2:W], OP.subtract)
            RUN("vector", "tensor_tensor", B3h[:, ci, 2:W], B3h[:, ci, 2:W],
                istd2b[:, 2:W], OP.mult)
        v_xn = GINC("vector", "ve")

        CUT(8)
        def mslice(hj, c0, c1):
            if hj < 32:
                return R1m[:, hj, c0:c1]
            return B4m[:, hj - 32, c0:c1]

        gu_free = {}
        te_gu_last = None
        ffn_state["first"] = {0: slab_state["consumed"][-2],
                              1: slab_state["consumed"][-1],
                              2: te_ffnln, 3: te_ffnln}
        ffn_issue(0, wg[0])
        ffn_issue(1, wu[0])
        ffn_issue(2, wg[1])
        ffn_issue(3, wu[1])
        for hj in range(NHT):
            base_g, ge_g = ffn_finish(2 * hj)
            base_u, ge_u = ffn_finish(2 * hj + 1)
            fw = [("ge", ge_g), ("ve", v_xn)]
            if hj == 0:
                fw.append(("ve", v_x1))
            if hj > 0:
                fw.append(("se", gu_free[("g", hj - 1)]))
            te_gmm = mm_matrix2(base_g, B3h, (pb[0], pb[1]), fw)
            ffn_state["consumed"][2 * hj] = cnt["te"]
            fw = [("ge", ge_u)]
            if hj > 0:
                fw.append(("se", gu_free[("u", hj - 1)]))
            te_umm = mm_matrix2(base_u, B3h, (pb[2], pb[3]), fw)
            ffn_state["consumed"][2 * hj + 1] = cnt["te"]
            te_gu_last = te_umm
            if hj + 2 < NHT:
                ffn_issue(2 * (hj + 2), wg[hj + 2])
                ffn_issue(2 * (hj + 2) + 1, wu[hj + 2])
            sg = cvec("s_gate", hj)
            su_ = cvec("s_up", hj)
            W8("scalar", "te", te_gmm)
            last_se_g = None
            last_se_u = None
            last_v = gu_free.get(("m", hj - 1))
            for c in range(2):
                if last_v is not None:
                    W8("scalar", "ve", last_v)
                s_g = RUN("scalar", "activation", scrGA[:], pb[c][:], AF.Silu,
                          scale=sg, inc="se")
                if c == 0:
                    W8("scalar", "te", te_umm)
                s_u = RUN("scalar", "activation",
                          mslice(hj, 512 * c, 512 * (c + 1)),
                          pb[2 + c][:], AF.Copy, scale=su_, inc="se")
                last_se_g, last_se_u = s_g, s_u
                W8("vector", "se", s_u)
                last_v = RUN("vector", "tensor_tensor",
                             mslice(hj, 512 * c, 512 * (c + 1)),
                             mslice(hj, 512 * c, 512 * (c + 1)), scrGA[:], OP.mult,
                             inc="ve")
            gu_free[("g", hj)] = last_se_g
            gu_free[("u", hj)] = last_se_u
            gu_free[("m", hj)] = last_v

        v_m_done = gu_free[("m", NHT - 1)]

        CUT(9)
        dn_free = {}
        load_dslab(wd[0], first_wait=te_gu_last)
        load_dslab(wd[1], first_wait=te_gu_last)
        for oj in range(16):
            base_d, ge_d = dslab_finish(oj)
            par = oj % 2
            banks = (pb[0], pb[1]) if par == 0 else (pb[2], pb[3])
            fw = [("ge", ge_d), ("ve", v_m_done)]
            if oj >= 2:
                fw.append(("ve", dn_free[oj - 2]))
            te_d = mm_matrix2(base_d, lambda ci, c0, c1: mslice(ci, c0 - 2, c1 - 2),
                              banks, fw, dslab=True)
            dslab_state["consumed"].append(cnt["te"])
            if oj + 2 < 16:
                load_dslab(wd[oj + 2])
            sd = cvec("s_down", oj)
            W8("vector", "te", te_d)
            for c in range(2):
                c0, c1, _ = mm_col[c]
                RUN("vector", "scalar_tensor_tensor", R2z[:, oj, c0:c1],
                    banks[c][:], sd, R2z[:, oj, c0:c1], OP.mult, OP.add)
            dn_free[oj] = GINC("vector", "ve")

        ost_free = [None, None]
        v_ost_prev = None
        for k in range(8):
            buf = k % 2
            if k == 0:
                W8("tensor", "ve", dn_free[14])
                W8("tensor", "ve", dn_free[15])
            else:
                W8("tensor", "ve", v_ost_prev)
            if ost_free[buf] is not None:
                W8("tensor", "qs%d" % buf, ost_free[buf])
            for ci in range(NCT):
                bank = pb[ci // 4]
                RUN("tensor", "transpose", bank[:, (ci % 4) * 128:(ci % 4 + 1) * 128],
                    R2z[:, ci, 2 + 128 * k:2 + 128 * (k + 1)], identf[:])
            te_tp = GINC("tensor", "te")
            W8("vector", "te", te_tp)
            for q in range(4):
                RUN("vector", "tensor_copy",
                    ostage[buf].rearrange("p (c t) -> p c t", c=16)[:, 4 * q:4 * (q + 1), :],
                    pb[q][:].rearrange("p (c t) -> p c t", c=4))
            v_ost = GINC("vector", "ve")
            v_ost_prev = v_ost
            W8("sync", "ve", v_ost)
            ost_free[buf] = DMA("sync", "qs%d" % buf,
                                y[128 * k:128 * (k + 1), :], ostage[buf])
        W8("sync", "qs0", cnt["qs0"])
        W8("sync", "qs1", cnt["qs1"])

    except _CutDone:
        for eng in ("tensor", "vector", "scalar", "gpsimd"):
            pass
        W8("sync", "te", cnt["te"])
        W8("sync", "ve", cnt["ve"])
        W8("sync", "se", cnt["se"])
        DMA("sync", "qs0", dbg[:, 0:NCT * W], R1.ap())
        DMA("sync", "qs1", dbg[:, NCT * W:NCT * W + NCT * W // 2],
            B3.ap().bitcast(F32))
        for k in range(8):
            DMA("sync", "qs%d" % (k % 2), y[128 * k:128 * (k + 1), :],
                R2.ap()[:, 2048 * k:2048 * (k + 1)])
        W8("sync", "qs0", cnt["qs0"])
        W8("sync", "qs1", cnt["qs1"])
    with nc.Block() as block:
        @block.sync
        def _(e):
            for f in prog["sync"]:
                f(e)

        @block.gpsimd
        def _(e):
            for f in prog["gpsimd"]:
                f(e)

        @block.tensor
        def _(e):
            for f in prog["tensor"]:
                f(e)

        @block.vector
        def _(e):
            for f in prog["vector"]:
                f(e)

        @block.scalar
        def _(e):
            for f in prog["scalar"]:
                f(e)

    es.close()
    return nc


def _swizzle(w, np=np):
    O_, C_ = w.shape
    a = w.reshape(O_ // 128, 128, C_ // 128, 128)
    a = a.transpose(0, 3, 2, 1)
    return np.ascontiguousarray(a.reshape(O_ // 128, 128, C_))


def _vec_tile(v, np=np):
    n = v.shape[0] // 128
    return v.reshape(n, 128).T


def kernel(**inp):
    x = np.asarray(inp["L0_attn_x"], dtype=np.float32)
    nc = _build()

    w_shared = {
        "wk": _swizzle(np.asarray(inp["L0_attn_tm_Wk"], np.float32)),
        "wv": _swizzle(np.asarray(inp["L0_attn_tm_Wv"], np.float32)),
        "wr": _swizzle(np.asarray(inp["L0_attn_tm_Wr"], np.float32)),
        "wo": _swizzle(np.asarray(inp["L0_attn_tm_Wout"], np.float32)),
        "wr2": _swizzle(np.asarray(inp["L0_attn_cm_Wr"], np.float32)),
        "wk2": _swizzle(np.asarray(inp["L0_attn_cm_Wk"], np.float32)),
        "wv2": _swizzle(np.asarray(inp["L0_attn_cm_Wv"], np.float32)),
        "wg": _swizzle(np.asarray(inp["L0_ffn_wq_gate"], np.int32)),
        "wu": _swizzle(np.asarray(inp["L0_ffn_wq_up"], np.int32)),
        "wd": _swizzle(np.asarray(inp["L0_ffn_wq_down"], np.int32)),
    }
    cbase = np.zeros((128, NCON), np.float32)
    for nm, key in (("mu_r", "L0_attn_tm_mu_r"), ("mu_k", "L0_attn_tm_mu_k"),
                    ("mu_v", "L0_attn_tm_mu_v"), ("cmu_r", "L0_attn_cm_mu_r"),
                    ("cmu_k", "L0_attn_cm_mu_k"), ("g1", "L0_attn_ln1_g"),
                    ("b1", "L0_attn_ln1_b"), ("g2", "L0_attn_ln2_g"),
                    ("b2", "L0_attn_ln2_b"), ("u", "L0_attn_tm_u"),
                    ("s_down", "L0_ffn_s_down"), ("s_gate", "L0_ffn_s_gate"),
                    ("s_up", "L0_ffn_s_up")):
        a, b = CC[nm]
        cbase[:, a:b] = _vec_tile(np.asarray(inp[key], np.float32))

    in_maps = []
    for c in range(8):
        b, h = c // 2, c % 2
        if h == 0:
            sl = np.concatenate([np.zeros((2, D), np.float32), x[b, 0:TPC]], 0)
        else:
            sl = x[b, TPC - 2:2 * TPC]
        ct = cbase.copy()
        ct[:, CC["hmask"][0]] = 0.0 if h == 0 else 1.0
        m = {"xs": np.ascontiguousarray(sl), "consts": ct}
        m.update(w_shared)
        in_maps.append(m)

    res = run_bass_kernel_spmd(nc, in_maps, core_ids=list(range(8)))
    out = np.empty((4, 2 * TPC, D), np.float32)
    for c in range(8):
        b, h = c // 2, c % 2
        out[b, h * TPC:(h + 1) * TPC] = np.asarray(res.results[c]["y"],
                                                   dtype=np.float32)
    return out


# version 03
# speedup vs baseline: 2.1947x; 2.1947x over previous
import numpy as np
from contextlib import ExitStack

import concourse.bass as bass
import concourse.mybir as mybir
from concourse.bass_utils import run_bass_kernel_spmd

F32 = mybir.dt.float32
BF16 = mybir.dt.bfloat16
I32 = mybir.dt.int32
AF = mybir.ActivationFunctionType
OP = mybir.AluOpType

D = 2048
HID = 5632
NCT = D // 128
NHT = HID // 128
W = 1026
TPC = 1024
CH = ((2, 514), (514, 1026))
LN2CH = ((0, 512), (512, 1024), (1024, 1026))
EPS = 1e-5
WKV_EPS = 1e-8

CC = {}
_c = 0
for _n, _w in (("mu_r", 16), ("mu_k", 16), ("mu_v", 16), ("cmu_r", 16),
               ("cmu_k", 16), ("g1", 16), ("b1", 16), ("g2", 16), ("b2", 16),
               ("u", 16), ("s_down", 16), ("s_gate", 44), ("s_up", 44),
               ("hmask", 1)):
    CC[_n] = (_c, _c + _w)
    _c += _w
NCON = _c


class _CutDone(Exception):
    pass


def _build(cut=None):
    nc = bass.Bass("TRN2", target_bir_lowering=False,
                   detect_race_conditions=False)
    xs = nc.dram_tensor("xs", [W, D], F32, kind="ExternalInput").ap()
    consts = nc.dram_tensor("consts", [128, NCON], F32, kind="ExternalInput").ap()
    wk = nc.dram_tensor("wk", [NCT, 128, D], F32, kind="ExternalInput").ap()
    wv = nc.dram_tensor("wv", [NCT, 128, D], F32, kind="ExternalInput").ap()
    wr = nc.dram_tensor("wr", [NCT, 128, D], F32, kind="ExternalInput").ap()
    wo = nc.dram_tensor("wo", [NCT, 128, D], F32, kind="ExternalInput").ap()
    wr2 = nc.dram_tensor("wr2", [NCT, 128, D], F32, kind="ExternalInput").ap()
    wk2 = nc.dram_tensor("wk2", [NCT, 128, D], F32, kind="ExternalInput").ap()
    wv2 = nc.dram_tensor("wv2", [NCT, 128, D], F32, kind="ExternalInput").ap()
    wg = nc.dram_tensor("wg", [NHT, 128, D], I32, kind="ExternalInput").ap()
    wu = nc.dram_tensor("wu", [NHT, 128, D], I32, kind="ExternalInput").ap()
    wd = nc.dram_tensor("wd", [NCT, 128, HID], I32, kind="ExternalInput").ap()
    zsave = nc.dram_tensor("zsave", [128, NCT * W], F32, kind="ExternalOutput").ap()
    y = nc.dram_tensor("y", [TPC, D], F32, kind="ExternalOutput").ap()
    dbg = (nc.dram_tensor("dbg", [128, 2 * NCT * W], F32,
                          kind="ExternalOutput").ap() if cut is not None else None)

    es = ExitStack()
    S = lambda *a, **k: es.enter_context(nc.sbuf_tensor(*a, **k))
    P = lambda *a, **k: es.enter_context(nc.psum_tensor(*a, **k))

    R1 = S("R1", [128, NCT * W], F32)
    R2 = S("R2", [128, NCT * W], F32)
    B3 = S("B3", [128, NCT * W], BF16)
    B4 = S("B4", [128, NCT * W], BF16)
    SLAB = S("SLAB", [128, 2 * D], BF16)
    ctile = S("ctile", [128, NCON], F32)
    dmu = S("dmu", [128, 5 * 16], F32)
    epsA = S("epsA", [128, 1], F32)
    nhA = S("nhA", [128, 1], F32)
    wepsA = S("wepsA", [128, 1], F32)
    identf = S("identf", [128, 128], F32)
    identb = S("identb", [128, 128], BF16)
    onesb = S("onesb", [128, 128], BF16)
    scrFA = S("scrFA", [128, 512], F32)
    scrGA = S("scrGA", [128, 512], BF16)
    stKU = S("stKU", [128, 16], F32)
    stE = S("stE", [128, 16], F32)
    stG = S("stG", [128, 16], BF16)
    stats = S("stats", [128, 24], F32)

    R1z = R1.ap().rearrange("p (c t) -> p c t", t=W)
    R1bf = R1.ap().bitcast(BF16)
    R1a = R1bf[:, 0:NCT * W].rearrange("p (c t) -> p c t", t=W)
    R1acc = R1bf[:, NCT * W:2 * NCT * W].rearrange("p (c t) -> p c t", t=W)
    R1x = R1.ap()[:, 0:8 * 2048].rearrange("p (k t) -> p k t", t=2048)
    R1m = R1bf[:, 0:32 * TPC].rearrange("p (h t) -> p h t", t=TPC)
    m2b = R1bf[:, 2 * NCT * W - 2 * W:2 * NCT * W - W]
    istd2b = R1bf[:, 2 * NCT * W - W:2 * NCT * W]
    R2z = R2.ap().rearrange("p (c t) -> p c t", t=W)
    R2bf = R2.ap().bitcast(BF16)
    R2a = R2bf[:, 0:NCT * W].rearrange("p (c t) -> p c t", t=W)
    R2b = R2bf[:, NCT * W:2 * NCT * W].rearrange("p (c t) -> p c t", t=W)
    B3h = B3.ap().rearrange("p (c t) -> p c t", t=W)
    B3ds = B3.ap().rearrange("p (s h) -> p s h", s=2)[:, :, 0:HID]
    B4h = B4.ap().rearrange("p (c t) -> p c t", t=W)
    B4f = B4.ap().bitcast(F32)
    xstage = [B4f[:, 0:2048], B4f[:, 2048:4096]]
    zzst = B4.ap()[:, 8192:8192 + 2048]
    dscr = B4.ap()[:, 10240:10240 + 2048]
    vdmp = B4.ap()[:, 12288:12288 + 2048]
    B4m = B4.ap()[:, 0:12 * TPC].rearrange("p (h t) -> p h t", t=TPC)
    ostage = [B4f[:, 0:2048], B4f[:, 2048:4096]]
    slab = SLAB.ap().rearrange("p (s d) -> p s d", s=2)
    ffn_bases = [slab[:, 0, :], slab[:, 1, :],
                 B4.ap()[:, 12288:14336], B4.ap()[:, 14336:16384]]
    scrH = scrFA.ap()

    def cvec(name, i=0):
        a, b = CC[name]
        return ctile[:, a + i:a + i + 1]

    def crange(name):
        a, b = CC[name]
        return ctile[:, a:b]

    pb = [P(f"pb{i}", [128, 512], F32) for i in range(6)]
    pz = [P(f"pz{i}", [128, 1024], BF16) for i in range(2)]

    prog = {k: [] for k in ("sync", "gpsimd", "tensor", "vector", "scalar")}
    sems = {}
    cnt = {}
    for sname in ("qs0", "qs1", "qg0", "qg1", "qg2", "qg3", "qc", "te", "ve", "se", "ge"):
        sems[sname] = es.enter_context(nc.semaphore(sname))
        cnt[sname] = 0

    last_ins = {k: None for k in prog}

    def emit(eng, fn):
        prog[eng].append(fn)

    def W8(eng, sem, val):
        emit(eng, lambda e, s=sems[sem], v=val: e.wait_ge(s, v))

    def RUN(eng, meth, *args, inc=None, **kw):
        if inc is not None:
            cnt[inc] += 16 if inc.startswith("q") else 1
        v = cnt[inc] if inc is not None else None
        def f(e, meth=meth, args=args, kw=kw, inc=inc, eng=eng):
            ins = getattr(e, meth)(*args, **kw)
            last_ins[eng] = ins
            if inc is not None:
                ins.then_inc(sems[inc], 16 if inc.startswith("q") else 1)
        emit(eng, f)
        return v

    def GINC(eng, sem):
        cnt[sem] += 1
        emit(eng, lambda e, s=sems[sem], eng=eng: last_ins[eng].then_inc(s, 1))
        return cnt[sem]

    def CUT(n):
        if cut is not None and cut == n:
            raise _CutDone()

    def DMA(eng, sem, out, in_, **kw):
        W8(eng, sem, cnt[sem])
        return RUN(eng, "dma_start", inc=sem, out=out, in_=in_, **kw)

    try:
        v_consts = DMA("gpsimd", "qc", ctile[:], consts)
        RUN("gpsimd", "memset", identf[:], 0.0)
        g_ident = RUN("gpsimd", "affine_select", inc="ge", out=identf[:],
                      in_=identf[:], compare_op=OP.not_equal, fill=1.0, base=0,
                      pattern=[[-1, 128]], channel_multiplier=1)

        W8("vector", "ge", g_ident)
        RUN("vector", "tensor_copy", identb[:], identf[:])
        emit("vector", lambda e: e.memset(onesb[:], 1.0))
        emit("vector", lambda e: e.memset(epsA[:], EPS))
        emit("vector", lambda e: e.memset(nhA[:], -0.5))
        emit("vector", lambda e: e.memset(wepsA[:], WKV_EPS))
        W8("vector", "qc", v_consts)
        for i, nm in enumerate(("mu_r", "mu_k", "mu_v", "cmu_r", "cmu_k")):
            RUN("vector", "tensor_scalar", dmu[:, 16 * i:16 * (i + 1)], crange(nm),
                1.0, None, OP.subtract)
        v_setup = GINC("vector", "ve")

        slab_state = {"idx": 0, "consumed": []}

        def load_slab(src_ap):
            i = slab_state["idx"]
            slab_state["idx"] += 1
            slot = i % 2
            qsem = "qg%d" % slot
            if i >= 2:
                W8("gpsimd", "te", slab_state["consumed"][i - 2])
            v = DMA("gpsimd", qsem, slab[:, slot, :], src_ap)
            return slab[:, slot, :], qsem, v

        ffn_state = {"pend": {}, "consumed": {}, "first": {}}

        def ffn_issue(i, src_ap):
            slot = i % 4
            qsem = "qg%d" % slot
            if i >= 4:
                W8("gpsimd", "te", ffn_state["consumed"][i - 4])
            elif slot in ffn_state["first"]:
                W8("gpsimd", "te", ffn_state["first"][slot])
            v = DMA("gpsimd", qsem, ffn_bases[slot], src_ap)
            ffn_state["pend"][i] = (slot, qsem, v)

        def ffn_finish(i):
            slot, qsem, v = ffn_state["pend"].pop(i)
            W8("vector", qsem, v)
            g = RUN("vector", "tensor_scalar", ffn_bases[slot], ffn_bases[slot],
                    128.0, None, OP.subtract, inc="ve")
            return ffn_bases[slot], g

        def slab_consumed():
            slab_state["consumed"].append(cnt["te"])

        dslab_state = {"idx": 0, "consumed": [], "pend": {}}

        def load_dslab(src_ap, first_wait=None):
            i = dslab_state["idx"]
            dslab_state["idx"] += 1
            slot = i % 2
            qsem = "qg%d" % slot
            if i >= 2:
                W8("gpsimd", "te", dslab_state["consumed"][i - 2])
            elif first_wait is not None:
                W8("gpsimd", "te", first_wait)
            v = DMA("gpsimd", qsem, B3ds[:, slot, 0:HID], src_ap)
            dslab_state["pend"][i] = (slot, qsem, v)
            return i

        def dslab_finish(i):
            slot, qsem, v = dslab_state["pend"].pop(i)
            W8("vector", qsem, v)
            g = RUN("vector", "tensor_scalar", B3ds[:, slot, 0:HID],
                    B3ds[:, slot, 0:HID], 128.0, None, OP.subtract, inc="ve")
            return B3ds[:, slot, 0:HID], g

        tiles = [(k, 2 + 128 * k, 128) for k in range(8)] + [(8, 0, 2)]
        tp_done = [None] * 9
        xst_v = [None] * 9
        vvar_hist = [None] * 9
        prev_pair = {}
        for k, row0, nrow in tiles:
            buf = k % 2
            st0 = 12 * (k % 2)
            if k >= 2:
                W8("sync", "te", tp_done[k - 2])
            xqs = "qs%d" % buf
            xst_v[k] = DMA("sync", xqs, xstage[buf][0:nrow, :],
                           xs[row0:row0 + nrow, :])
            xv = xstage[buf][0:nrow, :]
            zzv = zzst[0:nrow, :]
            su = stats[:, st0 + 0:st0 + 1][0:nrow, :]
            sq = stats[:, st0 + 1:st0 + 2][0:nrow, :]
            sm = stats[:, st0 + 2:st0 + 3][0:nrow, :]
            smm = stats[:, st0 + 3:st0 + 4][0:nrow, :]
            svar = stats[:, st0 + 4:st0 + 5][0:nrow, :]
            sii = stats[:, st0 + 5:st0 + 6][0:nrow, :]
            svz = stats[:, st0 + 6:st0 + 7][0:nrow, :]
            sistd = stats[:, st0 + 7:st0 + 8][0:nrow, :]
            sistd2 = stats[:, st0 + 8:st0 + 9][0:nrow, :]
            slnt = stats[:, st0 + 9:st0 + 10][0:nrow, :]
            W8("vector", xqs, xst_v[k])
            RUN("vector", "scalar_tensor_tensor", vdmp[0:nrow, :], xv, 1.0, xv,
                OP.bypass, OP.mult)
            RUN("vector", "reduce_sum", su, xv, mybir.AxisListType.X, inc="ve")
            v_t = RUN("vector", "reduce_sum", sq, vdmp[0:nrow, :],
                      mybir.AxisListType.X, inc="ve")
            W8("vector", "ve", v_t)
            v_t = RUN("vector", "tensor_scalar", sm, su, 1.0 / D, None, OP.mult,
                      inc="ve")
            W8("vector", "ve", v_t)
            v_t = RUN("vector", "tensor_tensor", smm, sm, sm, OP.mult, inc="ve")
            W8("vector", "ve", v_t)
            v_var = RUN("vector", "scalar_tensor_tensor", svar, sq, 1.0 / D, smm,
                        OP.mult, OP.subtract, inc="ve")
            W8("vector", "ve", v_var)
            vvar_hist[k] = v_var
            W8("scalar", "ve", v_var)
            se_t = RUN("scalar", "activation", slnt, svar, AF.Ln,
                       bias=epsA[0:nrow, :], inc="se")
            W8("scalar", "se", se_t)
            se_istd = RUN("scalar", "activation", sistd, slnt, AF.Exp,
                          scale=nhA[0:nrow, :], inc="se")
            W8("vector", "se", se_istd)
            v_t = RUN("vector", "tensor_tensor", sii, sistd, sistd, OP.mult,
                      inc="ve")
            W8("vector", "ve", v_t)
            v_vz = RUN("vector", "tensor_tensor", svz, svar, sii, OP.mult,
                       inc="ve")
            W8("scalar", "ve", v_vz)
            se_t = RUN("scalar", "activation", slnt, svz, AF.Ln,
                       bias=epsA[0:nrow, :], inc="se")
            W8("scalar", "se", se_t)
            se_istd2 = RUN("scalar", "activation", sistd2, slnt, AF.Exp,
                           scale=nhA[0:nrow, :], inc="se")
            RUN("vector", "tensor_scalar", xv, xv, sm, sistd, OP.subtract, OP.mult)
            W8("vector", "se", se_istd2)
            if k >= 1:
                W8("vector", "te", tp_done[k - 1])
            v_zz = RUN("vector", "tensor_scalar", zzv, xv, sistd2, None, OP.mult,
                       inc="ve")
            W8("tensor", "ve", v_zz)
            if k == 0:
                W8("tensor", "ge", g_ident)
                W8("tensor", "ve", v_setup)
            col0 = row0
            for g in range(4):
                bank = pb[g % 2]
                key = ("p0z", g % 2)
                if key in prev_pair:
                    W8("tensor", "ve", prev_pair[key])
                for j in range(4):
                    ci = 4 * g + j
                    RUN("tensor", "transpose", bank[0:128, j * 128:j * 128 + nrow],
                        xv[:, ci * 128:(ci + 1) * 128], identf[0:nrow, 0:nrow])
                t_g = GINC("tensor", "te")
                W8("vector", "te", t_g)
                ev = RUN("vector", "tensor_copy",
                         R1z[:, 4 * g:4 * g + 4, col0:col0 + nrow],
                         bank[:].rearrange("p (c t) -> p c t", c=4)[:, :, 0:nrow],
                         inc="ve")
                prev_pair[key] = ev
            for g in range(2):
                bank = pz[g]
                key = ("p0zz", g)
                if key in prev_pair:
                    W8("tensor", "se", prev_pair[key])
                for j in range(8):
                    ci = 8 * g + j
                    RUN("tensor", "transpose", bank[0:128, j * 128:j * 128 + nrow],
                        zzv[:, ci * 128:(ci + 1) * 128], identb[0:nrow, 0:nrow])
                t_g = GINC("tensor", "te")
                W8("scalar", "te", t_g)
                ev = RUN("scalar", "activation",
                         B3h[:, 8 * g:8 * g + 8, col0:col0 + nrow],
                         bank[:].rearrange("p (c t) -> p c t", c=8)[:, :, 0:nrow],
                         AF.Copy, inc="se")
                prev_pair[key] = ev
            tp_done[k] = cnt["te"]

        W8("vector", "se", cnt["se"])
        for ci in range(NCT):
            RUN("vector", "tensor_scalar", B3h[:, ci, :], B3h[:, ci, :],
                cvec("g1", ci), cvec("b1", ci), OP.mult, OP.add)
        for ci in range(NCT):
            RUN("vector", "tensor_scalar", B3h[:, ci, 0:2], B3h[:, ci, 0:2],
                cvec("hmask"), None, OP.mult)
        v_h = GINC("vector", "ve")

        CUT(1)
        W8("sync", "ve", v_h)
        qs_zoff = DMA("sync", "qs0", zsave, R1.ap())

        for ci in range(NCT):
            RUN("vector", "tensor_tensor", B4h[:, ci, 1:W], B3h[:, ci, 1:W],
                B3h[:, ci, 0:W - 1], OP.subtract)
        for ci in range(NCT):
            RUN("vector", "scalar_tensor_tensor", R2a[:, ci, 1:W], B4h[:, ci, 1:W],
                dmu[:, 16 * 1 + ci:16 * 1 + ci + 1], B3h[:, ci, 1:W], OP.mult, OP.add)
        v_xk = GINC("vector", "ve")
        for ci in range(NCT):
            RUN("vector", "scalar_tensor_tensor", R2b[:, ci, 1:W], B4h[:, ci, 1:W],
                dmu[:, 16 * 2 + ci:16 * 2 + ci + 1], B3h[:, ci, 1:W], OP.mult, OP.add)
        v_xv = GINC("vector", "ve")
        W8("vector", "qs0", qs_zoff)
        for ci in range(NCT):
            RUN("vector", "scalar_tensor_tensor", R1a[:, ci, 1:W], B4h[:, ci, 1:W],
                dmu[:, 16 * 0 + ci:16 * 0 + ci + 1], B3h[:, ci, 1:W], OP.mult, OP.add)
        v_xr = GINC("vector", "ve")

        mm_col = ((2, 514, 0), (514, 1026, 1))

        def mm_matrix(stat_base, mov3, banks, stray_col, first_waits=()):
            for wsem, wval in first_waits:
                W8("tensor", wsem, wval)
            for ci in range(NCT):
                st = stat_base[:, ci * 128:(ci + 1) * 128]
                for c0, c1, bo in mm_col:
                    RUN("tensor", "matmul", banks[bo][:, 0:c1 - c0], st,
                        mov3[:, ci, c0:c1], start=(ci == 0), stop=(ci == NCT - 1),
                        skip_group_check=True)
                if stray_col is not None:
                    RUN("tensor", "matmul", pb[4][:, stray_col:stray_col + 1], st,
                        mov3[:, ci, 1:2], start=(ci == 0), stop=(ci == NCT - 1),
                        skip_group_check=True)
            return GINC("tensor", "te")

        kv_free = {}
        te_kv_last = None
        u_all = crange("u")
        for oj in range(16):
            base_k, qsk, qk = load_slab(wk[oj])
            base_v, qsv, qv = load_slab(wv[oj])
            fw = [(qsk, qk), ("ve", v_xk)]
            if oj > 0:
                fw.append(("ve", kv_free[("k", oj - 1)]))
            te_k = mm_matrix(base_k, R2a, (pb[0], pb[1]), oj, fw)
            slab_consumed()
            fw = [(qsv, qv), ("ve", v_xv)]
            if oj > 0:
                fw.append(("ve", kv_free[("v", oj - 1)]))
            te_v = mm_matrix(base_v, R2b, (pb[2], pb[3]), 16 + oj, fw)
            slab_consumed()
            te_kv_last = te_v
            u_oj = cvec("u", oj)
            W8("scalar", "te", te_k)
            last_s6 = None
            for c in range(2):
                c0, c1, _ = mm_col[c]
                if last_s6 is not None:
                    W8("scalar", "ve", last_s6)
                s1 = RUN("scalar", "activation", scrFA[:], pb[c][:], AF.Exp,
                         bias=u_oj, inc="se")
                W8("vector", "se", s1)
                v2 = RUN("vector", "tensor_scalar", scrFA[:], scrFA[:], WKV_EPS,
                         None, OP.add, inc="ve")
                W8("scalar", "ve", v2)
                s3 = RUN("scalar", "activation", scrFA[:], scrFA[:], AF.Ln,
                         inc="se")
                W8("vector", "se", s3)
                v4 = RUN("vector", "tensor_tensor", scrFA[:], pb[c][:], scrFA[:],
                         OP.subtract, inc="ve")
                if c == 1:
                    kv_free[("k", oj)] = v4
                W8("scalar", "ve", v4)
                if c == 0:
                    W8("scalar", "te", te_v)
                s5 = RUN("scalar", "activation", scrGA[:], scrFA[:], AF.Exp,
                         bias=u_oj, inc="se")
                W8("vector", "se", s5)
                last_s6 = RUN("vector", "tensor_tensor", R1acc[:, oj, c0:c1],
                              scrGA[:], pb[2 + c][:], OP.mult, inc="ve")
            kv_free[("v", oj)] = last_s6

        CUT(2)
        W8("vector", "te", te_kv_last)
        W8("vector", "ve", kv_free[("v", 15)])
        RUN("vector", "tensor_tensor", stKU[:], pb[4][:, 0:16], u_all, OP.add)
        v_sku = GINC("vector", "ve")
        W8("scalar", "ve", v_sku)
        se_a = RUN("scalar", "activation", stE[:], stKU[:], AF.Exp, inc="se")
        W8("vector", "se", se_a)
        v_b = RUN("vector", "tensor_scalar", stE[:], stE[:], WKV_EPS, None, OP.add,
                  inc="ve")
        W8("scalar", "ve", v_b)
        se_st = RUN("scalar", "activation", stE[:], stE[:], AF.Ln, inc="se")
        W8("vector", "se", se_st)
        RUN("vector", "tensor_tensor", stKU[:], stKU[:], stE[:], OP.subtract)
        v_st = GINC("vector", "ve")
        W8("scalar", "ve", v_st)
        se_st = RUN("scalar", "activation", stG[:], stKU[:], AF.Exp, inc="se")
        W8("vector", "se", se_st)
        v_stray_kv = RUN("vector", "tensor_tensor", R1acc[:, :, 1], stG[:],
                         pb[4][:, 16:32], OP.mult, inc="ve")

        r_free = {}
        te_r_last = None
        for oj in range(16):
            base_r, qsr, qr = load_slab(wr[oj])
            par = oj % 2
            banks = (pb[0], pb[1]) if par == 0 else (pb[2], pb[3])
            fw = [(qsr, qr), ("ve", v_xr)]
            if oj == 0:
                fw.append(("ve", v_stray_kv))
            if oj >= 2:
                fw.append(("ve", r_free[oj - 2]))
            te_r = mm_matrix(base_r, R1a, banks, 32 + oj, fw)
            slab_consumed()
            te_r_last = te_r
            W8("scalar", "te", te_r)
            last_v = r_free.get(oj - 1)
            for c in range(2):
                c0, c1, _ = mm_col[c]
                if last_v is not None:
                    W8("scalar", "ve", last_v)
                s_sg = RUN("scalar", "activation", scrGA[:], banks[c][:],
                           AF.Sigmoid, inc="se")
                W8("vector", "se", s_sg)
                last_v = RUN("vector", "tensor_tensor", R1acc[:, oj, c0:c1],
                             R1acc[:, oj, c0:c1], scrGA[:], OP.mult, inc="ve")
            r_free[oj] = last_v
        W8("scalar", "te", te_r_last)
        W8("scalar", "ve", r_free[15])
        se_st = RUN("scalar", "activation", stG[:], pb[4][:, 32:48], AF.Sigmoid,
                    inc="se")
        W8("vector", "se", se_st)
        v_stray_r = RUN("vector", "tensor_tensor", R1acc[:, :, 1], R1acc[:, :, 1],
                        stG[:], OP.mult, inc="ve")
        v_acc_done = v_stray_r

        CUT(3)
        W8("sync", "qs0", qs_zoff)
        W8("sync", "te", te_kv_last)
        qs_zrel = DMA("sync", "qs1", R2.ap(), zsave)

        wo_free = {}
        te_wo_last = None
        for oj in range(16):
            base_o, qso, qo = load_slab(wo[oj])
            par = oj % 2
            banks = (pb[0], pb[1]) if par == 0 else (pb[2], pb[3])
            fw = [(qso, qo), ("ve", v_acc_done)]
            if oj == 0:
                fw.append(("ve", v_stray_r))
            if oj >= 2:
                fw.append(("ve", wo_free[oj - 2]))
            te_wo = mm_matrix(base_o, R1acc, banks, 48 + oj, fw)
            slab_consumed()
            te_wo_last = te_wo
            W8("vector", "te", te_wo)
            if oj == 0:
                W8("vector", "qs1", qs_zrel)
            for c in range(2):
                c0, c1, _ = mm_col[c]
                RUN("vector", "tensor_tensor", R2z[:, oj, c0:c1], R2z[:, oj, c0:c1],
                    banks[c][:], OP.add)
            wo_free[oj] = GINC("vector", "ve")
        W8("vector", "te", te_wo_last)
        v_z2 = RUN("vector", "tensor_tensor", R2z[:, :, 1], R2z[:, :, 1],
                   pb[4][:, 48:64], OP.add, inc="ve")

        CUT(4)
        W8("vector", "ve", v_z2)
        for ci in range(NCT):
            RUN("vector", "tensor_scalar", B3h[:, ci, :], R2z[:, ci, :], 1.0, None,
                OP.mult)
        for ci in range(NCT):
            RUN("vector", "tensor_tensor", B4h[:, ci, :], B3h[:, ci, :],
                B3h[:, ci, :], OP.mult)
        v_sq = GINC("vector", "ve")
        W8("tensor", "ve", v_sq)
        W8("tensor", "ve", wo_free[15])
        ln2banks = []
        for idx, (c0, c1) in enumerate(LN2CH):
            bsum = pb[idx][:, 0:c1 - c0] if idx < 2 else pb[4][:, 100:102]
            bsq = pb[2 + idx][:, 0:c1 - c0] if idx < 2 else pb[4][:, 102:104]
            for ci in range(NCT):
                RUN("tensor", "matmul", bsum, onesb[:], B3h[:, ci, c0:c1],
                    start=(ci == 0), stop=(ci == NCT - 1), skip_group_check=True)
            for ci in range(NCT):
                RUN("tensor", "matmul", bsq, onesb[:], B4h[:, ci, c0:c1],
                    start=(ci == 0), stop=(ci == NCT - 1), skip_group_check=True)
            ln2banks.append((bsum, bsq))
        te_ln2 = GINC("tensor", "te")

        W8("vector", "te", te_ln2)
        W8("scalar", "te", te_ln2)
        for idx, (c0, c1) in enumerate(LN2CH):
            bsum, bsq = ln2banks[idx]
            n = c1 - c0
            RUN("vector", "tensor_scalar", m2b[:, c0:c1], bsum, 1.0 / D, None,
                OP.mult)
            RUN("vector", "tensor_tensor", scrH[:, 0:n], m2b[:, c0:c1],
                m2b[:, c0:c1], OP.mult)
            v_t = RUN("vector", "scalar_tensor_tensor", scrH[:, 0:n], bsq, 1.0 / D,
                      scrH[:, 0:n], OP.mult, OP.subtract, inc="ve")
            W8("scalar", "ve", v_t)
            RUN("scalar", "activation", scrH[:, 0:n], scrH[:, 0:n], AF.Ln,
                bias=epsA[:])
            se_i = RUN("scalar", "activation", istd2b[:, c0:c1], scrH[:, 0:n],
                       AF.Exp, scale=nhA[:], inc="se")
            W8("vector", "se", se_i)
            se_ln2 = se_i
        W8("vector", "se", se_ln2)
        for ci in range(NCT):
            RUN("vector", "tensor_tensor", B3h[:, ci, 1:W], R2z[:, ci, 1:W],
                m2b[:, 1:W], OP.subtract)
            RUN("vector", "tensor_tensor", B3h[:, ci, 1:W], B3h[:, ci, 1:W],
                istd2b[:, 1:W], OP.mult)
            RUN("vector", "tensor_scalar", B3h[:, ci, 1:W], B3h[:, ci, 1:W],
                cvec("g2", ci), cvec("b2", ci), OP.mult, OP.add)
            RUN("vector", "tensor_scalar", B3h[:, ci, 1:2], B3h[:, ci, 1:2],
                cvec("hmask"), None, OP.mult)
        v_h2 = GINC("vector", "ve")

        for ci in range(NCT):
            RUN("vector", "tensor_tensor", B4h[:, ci, 2:W], B3h[:, ci, 2:W],
                B3h[:, ci, 1:W - 1], OP.subtract)
        for ci in range(NCT):
            RUN("vector", "scalar_tensor_tensor", R1a[:, ci, 2:W], B4h[:, ci, 2:W],
                dmu[:, 16 * 3 + ci:16 * 3 + ci + 1], B3h[:, ci, 2:W], OP.mult, OP.add)
        v_xr2 = GINC("vector", "ve")
        for ci in range(NCT):
            RUN("vector", "scalar_tensor_tensor", R1acc[:, ci, 2:W], B4h[:, ci, 2:W],
                dmu[:, 16 * 4 + ci:16 * 4 + ci + 1], B3h[:, ci, 2:W], OP.mult, OP.add)
        v_xk2 = GINC("vector", "ve")

        def mm_matrix2(stat_base, mov3, banks, first_waits=(), dslab=False):
            for wsem, wval in first_waits:
                W8("tensor", wsem, wval)
            nk = NHT if dslab else NCT
            for ci in range(nk):
                st = stat_base[:, ci * 128:(ci + 1) * 128]
                for c0, c1, bo in mm_col:
                    if dslab:
                        mov = mov3(ci, c0, c1)
                    else:
                        mov = mov3[:, ci, c0:c1]
                    RUN("tensor", "matmul", banks[bo][:, 0:c1 - c0], st, mov,
                        start=(ci == 0), stop=(ci == nk - 1),
                        skip_group_check=True)
            return GINC("tensor", "te")

        CUT(5)
        r2_free = {}
        for oj in range(16):
            base_q, qsq, q = load_slab(wr2[oj])
            par = oj % 2
            banks = (pb[0], pb[1]) if par == 0 else (pb[2], pb[3])
            fw = [(qsq, q), ("ve", v_xk2)]
            if oj >= 2:
                fw.append(("se", r2_free[oj - 2]))
            te_g = mm_matrix2(base_q, R1a, banks, fw)
            slab_consumed()
            W8("scalar", "te", te_g)
            for c in range(2):
                c0, c1, _ = mm_col[c]
                RUN("scalar", "activation", B3h[:, oj, c0:c1], banks[c][:],
                    AF.Sigmoid)
            r2_free[oj] = GINC("scalar", "se")
        k2_free = {}
        te_wk2_last = None
        for oj in range(16):
            base_q, qsq, q = load_slab(wk2[oj])
            par = oj % 2
            banks = (pb[0], pb[1]) if par == 0 else (pb[2], pb[3])
            fw = [(qsq, q), ("ve", v_xk2)]
            if oj == 0:
                fw.append(("se", r2_free[15]))
            if oj >= 2:
                fw.append(("ve", k2_free[oj - 2]))
            te_g = mm_matrix2(base_q, R1acc, banks, fw)
            slab_consumed()
            te_wk2_last = te_g
            W8("scalar", "te", te_g)
            last_v = k2_free.get(oj - 1)
            for c in range(2):
                c0, c1, _ = mm_col[c]
                if last_v is not None:
                    W8("scalar", "ve", last_v)
                s_g = RUN("scalar", "activation", scrGA[:], banks[c][:], AF.Relu,
                          inc="se")
                W8("vector", "se", s_g)
                last_v = RUN("vector", "tensor_tensor", B4h[:, oj, c0:c1],
                             scrGA[:], scrGA[:], OP.mult, inc="ve")
            k2_free[oj] = last_v

        W8("sync", "te", te_wk2_last)
        qs_x = {}
        for k in range(8):
            qs_x["qs%d" % (k % 2)] = DMA("sync", "qs%d" % (k % 2), R1x[:, k, :],
                                         xs[2 + 128 * k:2 + 128 * (k + 1), :])

        CUT(6)
        v2_free = {}
        xadd_free = {}
        for oj in range(16):
            base_q, qsq, q = load_slab(wv2[oj])
            par = oj % 2
            banks = (pb[0], pb[1]) if par == 0 else (pb[2], pb[3])
            fw = [(qsq, q), ("ve", k2_free[15])]
            if oj >= 2:
                fw.append(("ve", v2_free[oj - 2]))
            te_g = mm_matrix2(base_q, B4h, banks, fw)
            slab_consumed()
            W8("vector", "te", te_g)
            for c in range(2):
                c0, c1, _ = mm_col[c]
                RUN("vector", "tensor_tensor", scrH[:], B3h[:, oj, c0:c1],
                    banks[c][:], OP.mult)
                RUN("vector", "tensor_tensor", R2z[:, oj, c0:c1], R2z[:, oj, c0:c1],
                    scrH[:], OP.add)
            v2_free[oj] = GINC("vector", "ve")
            ci = oj
            W8("tensor", "ve", v2_free[oj])
            if oj == 0:
                for _qn, _qv in qs_x.items():
                    W8("tensor", _qn, _qv)
            else:
                W8("tensor", "ve", xadd_free[oj - 1])
            for kk in range(8):
                bank = pb[4] if kk < 4 else pb[5]
                RUN("tensor", "transpose", bank[:, (kk % 4) * 128:(kk % 4 + 1) * 128],
                    R1x[:, kk, ci * 128:(ci + 1) * 128], identf[:])
            te_tp = GINC("tensor", "te")
            W8("vector", "te", te_tp)
            for half in range(2):
                RUN("vector", "tensor_tensor",
                    R2z[:, ci, 2 + 512 * half:2 + 512 * (half + 1)],
                    R2z[:, ci, 2 + 512 * half:2 + 512 * (half + 1)],
                    pb[4 + half][:], OP.add)
            xadd_free[oj] = GINC("vector", "ve")
        v_x1 = xadd_free[15]

        CUT(7)
        W8("vector", "ve", v_x1)
        for ci in range(NCT):
            RUN("vector", "tensor_scalar", B3h[:, ci, 2:W], R2z[:, ci, 2:W], 1.0,
                None, OP.mult)
        for ci in range(NCT):
            RUN("vector", "tensor_tensor", B4h[:, ci, 2:W], B3h[:, ci, 2:W],
                B3h[:, ci, 2:W], OP.mult)
        v_x1sq = GINC("vector", "ve")
        W8("tensor", "ve", v_x1sq)
        ffnbanks = []
        for idx, (c0, c1) in enumerate(CH):
            bsum = pb[idx]
            bsq = pb[2 + idx]
            for ci in range(NCT):
                RUN("tensor", "matmul", bsum[:], onesb[:], B3h[:, ci, c0:c1],
                    start=(ci == 0), stop=(ci == NCT - 1), skip_group_check=True)
            for ci in range(NCT):
                RUN("tensor", "matmul", bsq[:], onesb[:], B4h[:, ci, c0:c1],
                    start=(ci == 0), stop=(ci == NCT - 1), skip_group_check=True)
            ffnbanks.append((bsum, bsq))
        te_ffnln = GINC("tensor", "te")
        W8("vector", "te", te_ffnln)
        W8("scalar", "te", te_ffnln)
        for idx, (c0, c1) in enumerate(CH):
            bsum, bsq = ffnbanks[idx]
            RUN("vector", "tensor_scalar", m2b[:, c0:c1], bsum[:], 1.0 / D, None,
                OP.mult)
            RUN("vector", "tensor_tensor", scrH[:], m2b[:, c0:c1], m2b[:, c0:c1],
                OP.mult)
            v_t = RUN("vector", "scalar_tensor_tensor", scrH[:], bsq[:], 1.0 / D,
                      scrH[:], OP.mult, OP.subtract, inc="ve")
            W8("scalar", "ve", v_t)
            RUN("scalar", "activation", scrH[:], scrH[:], AF.Ln, bias=epsA[:])
            se_i = RUN("scalar", "activation", istd2b[:, c0:c1], scrH[:], AF.Exp,
                       scale=nhA[:], inc="se")
            W8("vector", "se", se_i)
            se_ffnln = se_i
        W8("vector", "se", se_ffnln)
        for ci in range(NCT):
            RUN("vector", "tensor_tensor", B3h[:, ci, 2:W], R2z[:, ci, 2:W],
                m2b[:, 2:W], OP.subtract)
            RUN("vector", "tensor_tensor", B3h[:, ci, 2:W], B3h[:, ci, 2:W],
                istd2b[:, 2:W], OP.mult)
        v_xn = GINC("vector", "ve")

        CUT(8)
        def mslice(hj, c0, c1):
            if hj < 32:
                return R1m[:, hj, c0:c1]
            return B4m[:, hj - 32, c0:c1]

        gu_free = {}
        te_gu_last = None
        ffn_state["first"] = {0: slab_state["consumed"][-2],
                              1: slab_state["consumed"][-1],
                              2: te_ffnln, 3: te_ffnln}
        ffn_issue(0, wg[0])
        ffn_issue(1, wu[0])
        ffn_issue(2, wg[1])
        ffn_issue(3, wu[1])
        for hj in range(NHT):
            base_g, ge_g = ffn_finish(2 * hj)
            base_u, ge_u = ffn_finish(2 * hj + 1)
            fw = [("ve", ge_g), ("ve", v_xn)]
            if hj == 0:
                fw.append(("ve", v_x1))
            if hj > 0:
                fw.append(("se", gu_free[("g", hj - 1)]))
            te_gmm = mm_matrix2(base_g, B3h, (pb[0], pb[1]), fw)
            ffn_state["consumed"][2 * hj] = cnt["te"]
            fw = [("ve", ge_u)]
            if hj > 0:
                fw.append(("se", gu_free[("u", hj - 1)]))
            te_umm = mm_matrix2(base_u, B3h, (pb[2], pb[3]), fw)
            ffn_state["consumed"][2 * hj + 1] = cnt["te"]
            te_gu_last = te_umm
            if hj + 2 < NHT:
                ffn_issue(2 * (hj + 2), wg[hj + 2])
                ffn_issue(2 * (hj + 2) + 1, wu[hj + 2])
            sg = cvec("s_gate", hj)
            su_ = cvec("s_up", hj)
            W8("scalar", "te", te_gmm)
            last_se_g = None
            last_se_u = None
            last_v = gu_free.get(("m", hj - 1))
            for c in range(2):
                if last_v is not None:
                    W8("scalar", "ve", last_v)
                s_g = RUN("scalar", "activation", scrGA[:], pb[c][:], AF.Silu,
                          scale=sg, inc="se")
                if c == 0:
                    W8("scalar", "te", te_umm)
                s_u = RUN("scalar", "activation",
                          mslice(hj, 512 * c, 512 * (c + 1)),
                          pb[2 + c][:], AF.Copy, scale=su_, inc="se")
                last_se_g, last_se_u = s_g, s_u
                W8("vector", "se", s_u)
                last_v = RUN("vector", "tensor_tensor",
                             mslice(hj, 512 * c, 512 * (c + 1)),
                             mslice(hj, 512 * c, 512 * (c + 1)), scrGA[:], OP.mult,
                             inc="ve")
            gu_free[("g", hj)] = last_se_g
            gu_free[("u", hj)] = last_se_u
            gu_free[("m", hj)] = last_v

        v_m_done = gu_free[("m", NHT - 1)]

        CUT(9)
        dn_free = {}
        load_dslab(wd[0], first_wait=te_gu_last)
        load_dslab(wd[1], first_wait=te_gu_last)
        for oj in range(16):
            base_d, ge_d = dslab_finish(oj)
            par = oj % 2
            banks = (pb[0], pb[1]) if par == 0 else (pb[2], pb[3])
            fw = [("ve", ge_d), ("ve", v_m_done)]
            if oj >= 2:
                fw.append(("ve", dn_free[oj - 2]))
            te_d = mm_matrix2(base_d, lambda ci, c0, c1: mslice(ci, c0 - 2, c1 - 2),
                              banks, fw, dslab=True)
            dslab_state["consumed"].append(cnt["te"])
            if oj + 2 < 16:
                load_dslab(wd[oj + 2])
            sd = cvec("s_down", oj)
            W8("vector", "te", te_d)
            for c in range(2):
                c0, c1, _ = mm_col[c]
                RUN("vector", "scalar_tensor_tensor", R2z[:, oj, c0:c1],
                    banks[c][:], sd, R2z[:, oj, c0:c1], OP.mult, OP.add)
            dn_free[oj] = GINC("vector", "ve")

        ost_free = [None, None]
        v_ost_prev = None
        for k in range(8):
            buf = k % 2
            if k == 0:
                W8("tensor", "ve", dn_free[14])
                W8("tensor", "ve", dn_free[15])
            else:
                W8("tensor", "ve", v_ost_prev)
            if ost_free[buf] is not None:
                W8("tensor", "qs%d" % buf, ost_free[buf])
            for ci in range(NCT):
                bank = pb[ci // 4]
                RUN("tensor", "transpose", bank[:, (ci % 4) * 128:(ci % 4 + 1) * 128],
                    R2z[:, ci, 2 + 128 * k:2 + 128 * (k + 1)], identf[:])
            te_tp = GINC("tensor", "te")
            W8("vector", "te", te_tp)
            for q in range(4):
                RUN("vector", "tensor_copy",
                    ostage[buf].rearrange("p (c t) -> p c t", c=16)[:, 4 * q:4 * (q + 1), :],
                    pb[q][:].rearrange("p (c t) -> p c t", c=4))
            v_ost = GINC("vector", "ve")
            v_ost_prev = v_ost
            W8("sync", "ve", v_ost)
            ost_free[buf] = DMA("sync", "qs%d" % buf,
                                y[128 * k:128 * (k + 1), :], ostage[buf])
        W8("sync", "qs0", cnt["qs0"])
        W8("sync", "qs1", cnt["qs1"])

    except _CutDone:
        for eng in ("tensor", "vector", "scalar", "gpsimd"):
            pass
        W8("sync", "te", cnt["te"])
        W8("sync", "ve", cnt["ve"])
        W8("sync", "se", cnt["se"])
        DMA("sync", "qs0", dbg[:, 0:NCT * W], R1.ap())
        DMA("sync", "qs1", dbg[:, NCT * W:NCT * W + NCT * W // 2],
            B3.ap().bitcast(F32))
        for k in range(8):
            DMA("sync", "qs%d" % (k % 2), y[128 * k:128 * (k + 1), :],
                R2.ap()[:, 2048 * k:2048 * (k + 1)])
        W8("sync", "qs0", cnt["qs0"])
        W8("sync", "qs1", cnt["qs1"])
    with nc.Block() as block:
        @block.sync
        def _(e):
            for f in prog["sync"]:
                f(e)

        @block.gpsimd
        def _(e):
            for f in prog["gpsimd"]:
                f(e)

        @block.tensor
        def _(e):
            for f in prog["tensor"]:
                f(e)

        @block.vector
        def _(e):
            for f in prog["vector"]:
                f(e)

        @block.scalar
        def _(e):
            for f in prog["scalar"]:
                f(e)

    es.close()
    return nc


def _swizzle(w, np=np):
    O_, C_ = w.shape
    a = w.reshape(O_ // 128, 128, C_ // 128, 128)
    a = a.transpose(0, 3, 2, 1)
    return np.ascontiguousarray(a.reshape(O_ // 128, 128, C_))


def _vec_tile(v, np=np):
    n = v.shape[0] // 128
    return v.reshape(n, 128).T


def kernel(**inp):
    x = np.asarray(inp["L0_attn_x"], dtype=np.float32)
    nc = _build()

    w_shared = {
        "wk": _swizzle(np.asarray(inp["L0_attn_tm_Wk"], np.float32)),
        "wv": _swizzle(np.asarray(inp["L0_attn_tm_Wv"], np.float32)),
        "wr": _swizzle(np.asarray(inp["L0_attn_tm_Wr"], np.float32)),
        "wo": _swizzle(np.asarray(inp["L0_attn_tm_Wout"], np.float32)),
        "wr2": _swizzle(np.asarray(inp["L0_attn_cm_Wr"], np.float32)),
        "wk2": _swizzle(np.asarray(inp["L0_attn_cm_Wk"], np.float32)),
        "wv2": _swizzle(np.asarray(inp["L0_attn_cm_Wv"], np.float32)),
        "wg": _swizzle(np.asarray(inp["L0_ffn_wq_gate"], np.int32)),
        "wu": _swizzle(np.asarray(inp["L0_ffn_wq_up"], np.int32)),
        "wd": _swizzle(np.asarray(inp["L0_ffn_wq_down"], np.int32)),
    }
    cbase = np.zeros((128, NCON), np.float32)
    for nm, key in (("mu_r", "L0_attn_tm_mu_r"), ("mu_k", "L0_attn_tm_mu_k"),
                    ("mu_v", "L0_attn_tm_mu_v"), ("cmu_r", "L0_attn_cm_mu_r"),
                    ("cmu_k", "L0_attn_cm_mu_k"), ("g1", "L0_attn_ln1_g"),
                    ("b1", "L0_attn_ln1_b"), ("g2", "L0_attn_ln2_g"),
                    ("b2", "L0_attn_ln2_b"), ("u", "L0_attn_tm_u"),
                    ("s_down", "L0_ffn_s_down"), ("s_gate", "L0_ffn_s_gate"),
                    ("s_up", "L0_ffn_s_up")):
        a, b = CC[nm]
        cbase[:, a:b] = _vec_tile(np.asarray(inp[key], np.float32))

    in_maps = []
    for c in range(8):
        b, h = c // 2, c % 2
        if h == 0:
            sl = np.concatenate([np.zeros((2, D), np.float32), x[b, 0:TPC]], 0)
        else:
            sl = x[b, TPC - 2:2 * TPC]
        ct = cbase.copy()
        ct[:, CC["hmask"][0]] = 0.0 if h == 0 else 1.0
        m = {"xs": np.ascontiguousarray(sl), "consts": ct}
        m.update(w_shared)
        in_maps.append(m)

    res = run_bass_kernel_spmd(nc, in_maps, core_ids=list(range(8)))
    out = np.empty((4, 2 * TPC, D), np.float32)
    for c in range(8):
        b, h = c // 2, c % 2
        out[b, h * TPC:(h + 1) * TPC] = np.asarray(res.results[c]["y"],
                                                   dtype=np.float32)
    return out


# version 05
# speedup vs baseline: 2.4361x; 1.1100x over previous
import numpy as np
from contextlib import ExitStack

import concourse.bass as bass
import concourse.mybir as mybir
from concourse.bass_utils import run_bass_kernel_spmd

F32 = mybir.dt.float32
BF16 = mybir.dt.bfloat16
I32 = mybir.dt.int32
AF = mybir.ActivationFunctionType
OP = mybir.AluOpType

D = 2048
HID = 5632
NCT = D // 128
NHT = HID // 128
W = 1026
TPC = 1024
CH = ((2, 514), (514, 1026))
LN2CH = ((0, 512), (512, 1024), (1024, 1026))
EPS = 1e-5
WKV_EPS = 1e-8

CC = {}
_c = 0
for _n, _w in (("mu_r", 16), ("mu_k", 16), ("mu_v", 16), ("cmu_r", 16),
               ("cmu_k", 16), ("g1", 16), ("b1", 16), ("g2", 16), ("b2", 16),
               ("u", 16), ("s_down", 16), ("s_gate", 44), ("s_up", 44),
               ("hmask", 1), ("eps", 1), ("nh", 1), ("weps", 1)):
    CC[_n] = (_c, _c + _w)
    _c += _w
NCON = _c


class _CutDone(Exception):
    pass


def _build(cut=None):
    nc = bass.Bass("TRN2", target_bir_lowering=False,
                   detect_race_conditions=False)
    xs = nc.dram_tensor("xs", [W, D], F32, kind="ExternalInput").ap()
    consts = nc.dram_tensor("consts", [128, NCON], F32, kind="ExternalInput").ap()
    wk = nc.dram_tensor("wk", [NCT, 128, D], F32, kind="ExternalInput").ap()
    wv = nc.dram_tensor("wv", [NCT, 128, D], F32, kind="ExternalInput").ap()
    wr = nc.dram_tensor("wr", [NCT, 128, D], F32, kind="ExternalInput").ap()
    wo = nc.dram_tensor("wo", [NCT, 128, D], F32, kind="ExternalInput").ap()
    wr2 = nc.dram_tensor("wr2", [NCT, 128, D], F32, kind="ExternalInput").ap()
    wk2 = nc.dram_tensor("wk2", [NCT, 128, D], F32, kind="ExternalInput").ap()
    wv2 = nc.dram_tensor("wv2", [NCT, 128, D], F32, kind="ExternalInput").ap()
    wg = nc.dram_tensor("wg", [NHT, 128, D], I32, kind="ExternalInput").ap()
    wu = nc.dram_tensor("wu", [NHT, 128, D], I32, kind="ExternalInput").ap()
    wd = nc.dram_tensor("wd", [NCT, 128, HID], I32, kind="ExternalInput").ap()
    zsave = nc.dram_tensor("zsave", [128, NCT * W], F32, kind="ExternalOutput").ap()
    y = nc.dram_tensor("y", [TPC, D], F32, kind="ExternalOutput").ap()
    dbg = (nc.dram_tensor("dbg", [128, 2 * NCT * W], F32,
                          kind="ExternalOutput").ap() if cut is not None else None)

    es = ExitStack()
    S = lambda *a, **k: es.enter_context(nc.sbuf_tensor(*a, **k))
    P = lambda *a, **k: es.enter_context(nc.psum_tensor(*a, **k))

    R1 = S("R1", [128, NCT * W], F32)
    R2 = S("R2", [128, NCT * W], F32)
    B3 = S("B3", [128, NCT * W], BF16)
    B4 = S("B4", [128, NCT * W], BF16)
    SLAB = S("SLAB", [128, 2 * D], BF16)
    ctile = S("ctile", [128, NCON], F32)
    scrFA = S("scrFA", [128, 512], F32)
    scrFB = S("scrFB", [128, 512], F32)
    scrGA = S("scrGA", [128, 512], BF16)
    scrGB = S("scrGB", [128, 512], BF16)

    R1z = R1.ap().rearrange("p (c t) -> p c t", t=W)
    R1bf = R1.ap().bitcast(BF16)
    R1a = R1bf[:, 0:NCT * W].rearrange("p (c t) -> p c t", t=W)
    R1acc = R1bf[:, NCT * W:2 * NCT * W].rearrange("p (c t) -> p c t", t=W)
    R1x = R1.ap()[:, 0:8 * 2048].rearrange("p (k t) -> p k t", t=2048)
    R1m = R1bf[:, 0:32 * TPC].rearrange("p (h t) -> p h t", t=TPC)
    m2b = R1bf[:, 2 * NCT * W - 2 * W:2 * NCT * W - W]
    istd2b = R1bf[:, 2 * NCT * W - W:2 * NCT * W]
    R2z = R2.ap().rearrange("p (c t) -> p c t", t=W)
    R2bf = R2.ap().bitcast(BF16)
    R2a = R2bf[:, 0:NCT * W].rearrange("p (c t) -> p c t", t=W)
    R2b = R2bf[:, NCT * W:2 * NCT * W].rearrange("p (c t) -> p c t", t=W)
    B3h = B3.ap().rearrange("p (c t) -> p c t", t=W)
    B3ds = B3.ap().rearrange("p (s h) -> p s h", s=2)[:, :, 0:HID]
    B4h = B4.ap().rearrange("p (c t) -> p c t", t=W)
    B4f = B4.ap().bitcast(F32)
    xstage = [B4f[:, 0:2048], B4f[:, 2048:4096]]
    zzst = B4.ap()[:, 8192:8192 + 2048]
    dscr = B4.ap()[:, 10240:10240 + 2048]
    vdmp = B4.ap()[:, 12288:12288 + 2048]
    B4m = B4.ap()[:, 0:12 * TPC].rearrange("p (h t) -> p h t", t=TPC)
    ostage = [B4f[:, 0:2048], B4f[:, 2048:4096]]
    slab = SLAB.ap().rearrange("p (s d) -> p s d", s=2)
    ffn_bases = [slab[:, 0, :], slab[:, 1, :],
                 B4.ap()[:, 12288:14336], B4.ap()[:, 14336:16384]]
    scrH = scrFA.ap()
    identf = R2.ap()[:, 0:128]
    stats = R2.ap()[:, 128:152]
    identfB = scrFB.ap()[:, 0:128]
    onesb = scrGB.ap()[:, 128:256]
    identb = B4.ap()[:, 10240:10368]
    stKU = scrFB.ap()[:, 0:16]
    stE = scrFB.ap()[:, 16:32]
    stG = scrGB.ap()[:, 0:16]
    epsA = ctile[:, CC["eps"][0]:CC["eps"][0] + 1]
    nhA = ctile[:, CC["nh"][0]:CC["nh"][0] + 1]
    wepsA = ctile[:, CC["weps"][0]:CC["weps"][0] + 1]

    def cvec(name, i=0):
        a, b = CC[name]
        return ctile[:, a + i:a + i + 1]

    def crange(name):
        a, b = CC[name]
        return ctile[:, a:b]

    pb = [P(f"pb{i}", [128, 512], F32) for i in range(6)]
    pz = [P(f"pz{i}", [128, 1024], BF16) for i in range(2)]

    prog = {k: [] for k in ("sync", "gpsimd", "tensor", "vector", "scalar")}
    sems = {}
    cnt = {}
    for sname in ("qs0", "qs1", "qg0", "qg1", "qg2", "qg3", "qc", "te", "ve", "se", "ge"):
        sems[sname] = es.enter_context(nc.semaphore(sname))
        cnt[sname] = 0

    last_ins = {k: None for k in prog}

    def emit(eng, fn):
        prog[eng].append(fn)

    def W8(eng, sem, val):
        emit(eng, lambda e, s=sems[sem], v=val: e.wait_ge(s, v))

    def RUN(eng, meth, *args, inc=None, **kw):
        if inc is not None:
            cnt[inc] += 16 if inc.startswith("q") else 1
        v = cnt[inc] if inc is not None else None
        def f(e, meth=meth, args=args, kw=kw, inc=inc, eng=eng):
            ins = getattr(e, meth)(*args, **kw)
            last_ins[eng] = ins
            if inc is not None:
                ins.then_inc(sems[inc], 16 if inc.startswith("q") else 1)
        emit(eng, f)
        return v

    def GINC(eng, sem):
        cnt[sem] += 1
        emit(eng, lambda e, s=sems[sem], eng=eng: last_ins[eng].then_inc(s, 1))
        return cnt[sem]

    def CUT(n):
        if cut is not None and cut == n:
            raise _CutDone()

    def DMA(eng, sem, out, in_, **kw):
        W8(eng, sem, cnt[sem])
        return RUN(eng, "dma_start", inc=sem, out=out, in_=in_, **kw)

    try:
        v_consts = DMA("gpsimd", "qc", ctile[:], consts)
        RUN("gpsimd", "memset", identf[:], 0.0)
        g_ident = RUN("gpsimd", "affine_select", inc="ge", out=identf[:],
                      in_=identf[:], compare_op=OP.not_equal, fill=1.0, base=0,
                      pattern=[[-1, 128]], channel_multiplier=1)

        W8("vector", "ge", g_ident)
        RUN("vector", "tensor_copy", identb[:], identf[:])
        W8("vector", "qc", v_consts)
        for nm in ("mu_r", "mu_k", "mu_v", "cmu_r", "cmu_k"):
            RUN("vector", "tensor_scalar", crange(nm), crange(nm),
                1.0, None, OP.subtract)
        v_setup = GINC("vector", "ve")

        slab_state = {"idx": 0, "consumed": []}

        def load_slab(src_ap):
            i = slab_state["idx"]
            slab_state["idx"] += 1
            slot = i % 2
            qsem = "qg%d" % slot
            if i >= 2:
                W8("gpsimd", "te", slab_state["consumed"][i - 2])
            v = DMA("gpsimd", qsem, slab[:, slot, :], src_ap)
            return slab[:, slot, :], qsem, v

        ffn_state = {"pend": {}, "consumed": {}, "first": {}}

        def ffn_issue(i, src_ap):
            slot = i % 4
            qsem = "qg%d" % slot
            if i >= 4:
                W8("gpsimd", "te", ffn_state["consumed"][i - 4])
            elif slot in ffn_state["first"]:
                W8("gpsimd", "te", ffn_state["first"][slot])
            v = DMA("gpsimd", qsem, ffn_bases[slot], src_ap)
            ffn_state["pend"][i] = (slot, qsem, v)

        def ffn_finish(i):
            slot, qsem, v = ffn_state["pend"].pop(i)
            W8("vector", qsem, v)
            g = RUN("vector", "tensor_scalar", ffn_bases[slot], ffn_bases[slot],
                    128.0, None, OP.subtract, inc="ve")
            return ffn_bases[slot], g

        def slab_consumed():
            slab_state["consumed"].append(cnt["te"])

        dslab_state = {"idx": 0, "consumed": [], "pend": {}}

        def load_dslab(src_ap, first_wait=None):
            i = dslab_state["idx"]
            dslab_state["idx"] += 1
            slot = i % 2
            qsem = "qg%d" % slot
            if i >= 2:
                W8("gpsimd", "te", dslab_state["consumed"][i - 2])
            elif first_wait is not None:
                W8("gpsimd", "te", first_wait)
            v = DMA("gpsimd", qsem, B3ds[:, slot, 0:HID], src_ap)
            dslab_state["pend"][i] = (slot, qsem, v)
            return i

        def dslab_finish(i):
            slot, qsem, v = dslab_state["pend"].pop(i)
            W8("vector", qsem, v)
            g = RUN("vector", "tensor_scalar", B3ds[:, slot, 0:HID],
                    B3ds[:, slot, 0:HID], 128.0, None, OP.subtract, inc="ve")
            return B3ds[:, slot, 0:HID], g

        tiles = [(k, 2 + 128 * k, 128) for k in range(8)] + [(8, 0, 2)]
        tp_done = [None] * 9
        xst_v = [None] * 9
        vvar_hist = [None] * 9
        prev_pair = {}
        for k, row0, nrow in tiles:
            buf = k % 2
            st0 = 12 * (k % 2)
            if k >= 2:
                W8("sync", "te", tp_done[k - 2])
            xqs = "qs%d" % buf
            xst_v[k] = DMA("sync", xqs, xstage[buf][0:nrow, :],
                           xs[row0:row0 + nrow, :])
            xv = xstage[buf][0:nrow, :]
            zzv = zzst[0:nrow, :]
            su = stats[:, st0 + 0:st0 + 1][0:nrow, :]
            sq = stats[:, st0 + 1:st0 + 2][0:nrow, :]
            sm = stats[:, st0 + 2:st0 + 3][0:nrow, :]
            smm = stats[:, st0 + 3:st0 + 4][0:nrow, :]
            svar = stats[:, st0 + 4:st0 + 5][0:nrow, :]
            sii = stats[:, st0 + 5:st0 + 6][0:nrow, :]
            svz = stats[:, st0 + 6:st0 + 7][0:nrow, :]
            sistd = stats[:, st0 + 7:st0 + 8][0:nrow, :]
            sistd2 = stats[:, st0 + 8:st0 + 9][0:nrow, :]
            slnt = stats[:, st0 + 9:st0 + 10][0:nrow, :]
            W8("vector", xqs, xst_v[k])
            RUN("vector", "scalar_tensor_tensor", vdmp[0:nrow, :], xv, 1.0, xv,
                OP.bypass, OP.mult)
            RUN("vector", "reduce_sum", su, xv, mybir.AxisListType.X, inc="ve")
            v_t = RUN("vector", "reduce_sum", sq, vdmp[0:nrow, :],
                      mybir.AxisListType.X, inc="ve")
            W8("vector", "ve", v_t)
            v_t = RUN("vector", "tensor_scalar", sm, su, 1.0 / D, None, OP.mult,
                      inc="ve")
            W8("vector", "ve", v_t)
            v_t = RUN("vector", "tensor_tensor", smm, sm, sm, OP.mult, inc="ve")
            W8("vector", "ve", v_t)
            v_var = RUN("vector", "scalar_tensor_tensor", svar, sq, 1.0 / D, smm,
                        OP.mult, OP.subtract, inc="ve")
            W8("vector", "ve", v_var)
            vvar_hist[k] = v_var
            W8("scalar", "ve", v_var)
            se_t = RUN("scalar", "activation", slnt, svar, AF.Ln,
                       bias=epsA[0:nrow, :], inc="se")
            W8("scalar", "se", se_t)
            se_istd = RUN("scalar", "activation", sistd, slnt, AF.Exp,
                          scale=nhA[0:nrow, :], inc="se")
            W8("vector", "se", se_istd)
            v_t = RUN("vector", "tensor_tensor", sii, sistd, sistd, OP.mult,
                      inc="ve")
            W8("vector", "ve", v_t)
            v_vz = RUN("vector", "tensor_tensor", svz, svar, sii, OP.mult,
                       inc="ve")
            W8("scalar", "ve", v_vz)
            se_t = RUN("scalar", "activation", slnt, svz, AF.Ln,
                       bias=epsA[0:nrow, :], inc="se")
            W8("scalar", "se", se_t)
            se_istd2 = RUN("scalar", "activation", sistd2, slnt, AF.Exp,
                           scale=nhA[0:nrow, :], inc="se")
            RUN("vector", "tensor_scalar", xv, xv, sm, sistd, OP.subtract, OP.mult)
            W8("vector", "se", se_istd2)
            if k >= 1:
                W8("vector", "te", tp_done[k - 1])
            v_zz = RUN("vector", "tensor_scalar", zzv, xv, sistd2, None, OP.mult,
                       inc="ve")
            W8("tensor", "ve", v_zz)
            if k == 0:
                W8("tensor", "ge", g_ident)
                W8("tensor", "ve", v_setup)
            col0 = row0
            for g in range(4):
                bank = pb[g % 2]
                key = ("p0z", g % 2)
                if key in prev_pair:
                    W8("tensor", "ve", prev_pair[key])
                for j in range(4):
                    ci = 4 * g + j
                    RUN("tensor", "transpose", bank[0:128, j * 128:j * 128 + nrow],
                        xv[:, ci * 128:(ci + 1) * 128], identf[0:nrow, 0:nrow])
                t_g = GINC("tensor", "te")
                W8("vector", "te", t_g)
                ev = RUN("vector", "tensor_copy",
                         R1z[:, 4 * g:4 * g + 4, col0:col0 + nrow],
                         bank[:].rearrange("p (c t) -> p c t", c=4)[:, :, 0:nrow],
                         inc="ve")
                prev_pair[key] = ev
            for g in range(2):
                bank = pz[g]
                key = ("p0zz", g)
                if key in prev_pair:
                    W8("tensor", "se", prev_pair[key])
                for j in range(8):
                    ci = 8 * g + j
                    RUN("tensor", "transpose", bank[0:128, j * 128:j * 128 + nrow],
                        zzv[:, ci * 128:(ci + 1) * 128], identb[0:nrow, 0:nrow])
                t_g = GINC("tensor", "te")
                W8("scalar", "te", t_g)
                ev = RUN("scalar", "activation",
                         B3h[:, 8 * g:8 * g + 8, col0:col0 + nrow],
                         bank[:].rearrange("p (c t) -> p c t", c=8)[:, :, 0:nrow],
                         AF.Copy, inc="se")
                prev_pair[key] = ev
            tp_done[k] = cnt["te"]

        W8("vector", "se", cnt["se"])
        for ci in range(NCT):
            RUN("vector", "tensor_scalar", B3h[:, ci, :], B3h[:, ci, :],
                cvec("g1", ci), cvec("b1", ci), OP.mult, OP.add)
        for ci in range(NCT):
            RUN("vector", "tensor_scalar", B3h[:, ci, 0:2], B3h[:, ci, 0:2],
                cvec("hmask"), None, OP.mult)
        v_h = GINC("vector", "ve")

        CUT(1)
        W8("sync", "ve", v_h)
        qs_zoff = DMA("sync", "qs0", zsave, R1.ap())

        vmix = {}
        for a0, a1, tag in ((1, 514, "A"), (514, W, "B")):
            for ci in range(NCT):
                RUN("vector", "tensor_tensor", B4h[:, ci, a0:a1], B3h[:, ci, a0:a1],
                    B3h[:, ci, a0 - 1:a1 - 1], OP.subtract)
            for ci in range(NCT):
                RUN("vector", "scalar_tensor_tensor", R2a[:, ci, a0:a1],
                    B4h[:, ci, a0:a1], cvec("mu_k", ci), B3h[:, ci, a0:a1],
                    OP.mult, OP.add)
            vmix[("xk", tag)] = GINC("vector", "ve")
            for ci in range(NCT):
                RUN("vector", "scalar_tensor_tensor", R2b[:, ci, a0:a1],
                    B4h[:, ci, a0:a1], cvec("mu_v", ci), B3h[:, ci, a0:a1],
                    OP.mult, OP.add)
            vmix[("xv", tag)] = GINC("vector", "ve")
            if tag == "A":
                W8("vector", "qs0", qs_zoff)
            for ci in range(NCT):
                RUN("vector", "scalar_tensor_tensor", R1a[:, ci, a0:a1],
                    B4h[:, ci, a0:a1], cvec("mu_r", ci), B3h[:, ci, a0:a1],
                    OP.mult, OP.add)
            vmix[("xr", tag)] = GINC("vector", "ve")
        v_xk = vmix[("xk", "B")]
        v_xv = vmix[("xv", "B")]
        v_xr = vmix[("xr", "B")]

        mm_col = ((2, 514, 0), (514, 1026, 1))

        def mm_matrix(stat_base, mov3, banks, stray_col, first_waits=(),
                      waitsB=()):
            for wsem, wval in first_waits:
                W8("tensor", wsem, wval)
            c0, c1, bo = mm_col[0]
            for ci in range(NCT):
                st = stat_base[:, ci * 128:(ci + 1) * 128]
                RUN("tensor", "matmul", banks[bo][:, 0:c1 - c0], st,
                    mov3[:, ci, c0:c1], start=(ci == 0), stop=(ci == NCT - 1),
                    skip_group_check=True)
                if stray_col is not None:
                    RUN("tensor", "matmul", pb[4][:, stray_col:stray_col + 1], st,
                        mov3[:, ci, 1:2], start=(ci == 0), stop=(ci == NCT - 1),
                        skip_group_check=True)
            teA = GINC("tensor", "te")
            for wsem, wval in waitsB:
                W8("tensor", wsem, wval)
            c0, c1, bo = mm_col[1]
            for ci in range(NCT):
                st = stat_base[:, ci * 128:(ci + 1) * 128]
                RUN("tensor", "matmul", banks[bo][:, 0:c1 - c0], st,
                    mov3[:, ci, c0:c1], start=(ci == 0), stop=(ci == NCT - 1),
                    skip_group_check=True)
            teB = GINC("tensor", "te")
            return teA, teB

        kv_free = {}
        te_kv_last = None
        prev_s6 = [None, None]
        u_all = crange("u")
        for oj in range(16):
            base_k, qsk, qk = load_slab(wk[oj])
            base_v, qsv, qv = load_slab(wv[oj])
            fw = [(qsk, qk), ("ve", vmix[("xk", "A")])]
            if oj > 0:
                fw.append(("ve", kv_free[("k", oj - 1)]))
            te_kA, te_kB = mm_matrix(base_k, R2a, (pb[0], pb[1]), oj, fw,
                                     waitsB=[("ve", v_xk)])
            slab_consumed()
            fw = [(qsv, qv), ("ve", vmix[("xv", "A")])]
            if oj > 0:
                fw.append(("ve", kv_free[("v", oj - 1)]))
            te_vA, te_vB = mm_matrix(base_v, R2b, (pb[2], pb[3]), 16 + oj, fw,
                                     waitsB=[("ve", v_xv)])
            slab_consumed()
            te_v = te_vB
            te_kv_last = te_vB
            u_oj = cvec("u", oj)
            scrF = (scrFA, scrFB)
            scrG = (scrGA, scrGB)
            s1 = [None, None]; s3 = [None, None]; s5 = [None, None]
            v2 = [None, None]; v4 = [None, None]; v6 = [None, None]
            for c in range(2):
                W8("scalar", "te", (te_kA, te_kB)[c])
                s1[c] = RUN("scalar", "activation", scrF[c][:], pb[c][:], AF.Exp,
                            bias=u_oj, inc="se")
            for c in range(2):
                W8("vector", "se", s1[c])
                v2[c] = RUN("vector", "tensor_scalar", scrF[c][:], scrF[c][:],
                            WKV_EPS, None, OP.add, inc="ve")
            for c in range(2):
                W8("scalar", "ve", v2[c])
                s3[c] = RUN("scalar", "activation", scrF[c][:], scrF[c][:], AF.Ln,
                            inc="se")
            for c in range(2):
                W8("vector", "se", s3[c])
                v4[c] = RUN("vector", "tensor_tensor", scrF[c][:], pb[c][:],
                            scrF[c][:], OP.subtract, inc="ve")
            kv_free[("k", oj)] = v4[1]
            for c in range(2):
                W8("scalar", "ve", v4[c])
                if oj > 0:
                    W8("scalar", "ve", prev_s6[c])
                s5[c] = RUN("scalar", "activation", scrG[c][:], scrF[c][:], AF.Exp,
                            bias=u_oj, inc="se")
            for c in range(2):
                c0, c1, _ = mm_col[c]
                W8("vector", "te", (te_vA, te_vB)[c])
                W8("vector", "se", s5[c])
                v6[c] = RUN("vector", "tensor_tensor", R1acc[:, oj, c0:c1],
                            scrG[c][:], pb[2 + c][:], OP.mult, inc="ve")
            prev_s6 = v6
            last_s6 = v6[1]
            kv_free[("v", oj)] = last_s6

        CUT(2)
        W8("vector", "te", te_kv_last)
        W8("vector", "ve", kv_free[("v", 15)])
        RUN("vector", "tensor_tensor", stKU[:], pb[4][:, 0:16], u_all, OP.add)
        v_sku = GINC("vector", "ve")
        W8("scalar", "ve", v_sku)
        se_a = RUN("scalar", "activation", stE[:], stKU[:], AF.Exp, inc="se")
        W8("vector", "se", se_a)
        v_b = RUN("vector", "tensor_scalar", stE[:], stE[:], WKV_EPS, None, OP.add,
                  inc="ve")
        W8("scalar", "ve", v_b)
        se_st = RUN("scalar", "activation", stE[:], stE[:], AF.Ln, inc="se")
        W8("vector", "se", se_st)
        RUN("vector", "tensor_tensor", stKU[:], stKU[:], stE[:], OP.subtract)
        v_st = GINC("vector", "ve")
        W8("scalar", "ve", v_st)
        se_st = RUN("scalar", "activation", stG[:], stKU[:], AF.Exp, inc="se")
        W8("vector", "se", se_st)
        v_stray_kv = RUN("vector", "tensor_tensor", R1acc[:, :, 1], stG[:],
                         pb[4][:, 16:32], OP.mult, inc="ve")

        r_free = {}
        r_bank_free = {}
        prev_rmul = [None, None]
        te_r_last = None
        for oj in range(16):
            base_r, qsr, qr = load_slab(wr[oj])
            par = oj % 2
            banks = (pb[0], pb[1]) if par == 0 else (pb[2], pb[3])
            fw = [(qsr, qr), ("ve", v_xr)]
            if oj == 0:
                fw.append(("ve", v_stray_kv))
            if oj >= 2:
                fw.append(("se", r_bank_free[oj - 2]))
            te_rA, te_rB = mm_matrix(base_r, R1a, banks, 32 + oj, fw)
            slab_consumed()
            te_r_last = te_rB
            s_sg = [None, None]
            for c in range(2):
                W8("scalar", "te", (te_rA, te_rB)[c])
                if prev_rmul[c] is not None:
                    W8("scalar", "ve", prev_rmul[c])
                s_sg[c] = RUN("scalar", "activation", (scrGA, scrGB)[c][:],
                              banks[c][:], AF.Sigmoid, inc="se")
            r_bank_free[oj] = s_sg[1]
            for c in range(2):
                c0, c1, _ = mm_col[c]
                W8("vector", "se", s_sg[c])
                prev_rmul[c] = RUN("vector", "tensor_tensor", R1acc[:, oj, c0:c1],
                                   R1acc[:, oj, c0:c1], (scrGA, scrGB)[c][:],
                                   OP.mult, inc="ve")
            r_free[oj] = prev_rmul[1]
        W8("scalar", "te", te_r_last)
        W8("scalar", "ve", r_free[15])
        se_st = RUN("scalar", "activation", stG[:], pb[4][:, 32:48], AF.Sigmoid,
                    inc="se")
        W8("vector", "se", se_st)
        v_stray_r = RUN("vector", "tensor_tensor", R1acc[:, :, 1], R1acc[:, :, 1],
                        stG[:], OP.mult, inc="ve")
        v_acc_done = v_stray_r

        CUT(3)
        W8("sync", "qs0", qs_zoff)
        W8("sync", "te", te_kv_last)
        qs_zrel = DMA("sync", "qs1", R2.ap(), zsave)

        wo_free = {}
        te_wo_last = None
        for oj in range(16):
            base_o, qso, qo = load_slab(wo[oj])
            par = oj % 2
            banks = (pb[0], pb[1]) if par == 0 else (pb[2], pb[3])
            fw = [(qso, qo), ("ve", v_acc_done)]
            if oj == 0:
                fw.append(("ve", v_stray_r))
            if oj >= 2:
                fw.append(("ve", wo_free[oj - 2]))
            te_woA, te_woB = mm_matrix(base_o, R1acc, banks, 48 + oj, fw)
            slab_consumed()
            te_wo_last = te_woB
            if oj == 0:
                W8("vector", "qs1", qs_zrel)
            for c in range(2):
                c0, c1, _ = mm_col[c]
                W8("vector", "te", (te_woA, te_woB)[c])
                RUN("vector", "tensor_tensor", R2z[:, oj, c0:c1], R2z[:, oj, c0:c1],
                    banks[c][:], OP.add)
            wo_free[oj] = GINC("vector", "ve")
        W8("vector", "te", te_wo_last)
        v_z2 = RUN("vector", "tensor_tensor", R2z[:, :, 1], R2z[:, :, 1],
                   pb[4][:, 48:64], OP.add, inc="ve")

        CUT(4)
        W8("vector", "ve", v_z2)
        RUN("vector", "memset", onesb[:], 1.0)
        for ci in range(NCT):
            RUN("vector", "tensor_scalar", B3h[:, ci, :], R2z[:, ci, :], 1.0, None,
                OP.mult)
        for ci in range(NCT):
            RUN("vector", "tensor_tensor", B4h[:, ci, :], B3h[:, ci, :],
                B3h[:, ci, :], OP.mult)
        v_sq = GINC("vector", "ve")
        W8("tensor", "ve", v_sq)
        W8("tensor", "ve", wo_free[15])
        ln2banks = []
        for idx, (c0, c1) in enumerate(LN2CH):
            bsum = pb[idx][:, 0:c1 - c0] if idx < 2 else pb[4][:, 100:102]
            bsq = pb[2 + idx][:, 0:c1 - c0] if idx < 2 else pb[4][:, 102:104]
            for ci in range(NCT):
                RUN("tensor", "matmul", bsum, onesb[:], B3h[:, ci, c0:c1],
                    start=(ci == 0), stop=(ci == NCT - 1), skip_group_check=True)
            for ci in range(NCT):
                RUN("tensor", "matmul", bsq, onesb[:], B4h[:, ci, c0:c1],
                    start=(ci == 0), stop=(ci == NCT - 1), skip_group_check=True)
            ln2banks.append((bsum, bsq))
        te_ln2 = GINC("tensor", "te")

        W8("vector", "te", te_ln2)
        W8("scalar", "te", te_ln2)
        for idx, (c0, c1) in enumerate(LN2CH):
            bsum, bsq = ln2banks[idx]
            n = c1 - c0
            RUN("vector", "tensor_scalar", m2b[:, c0:c1], bsum, 1.0 / D, None,
                OP.mult)
            RUN("vector", "tensor_tensor", scrH[:, 0:n], m2b[:, c0:c1],
                m2b[:, c0:c1], OP.mult)
            v_t = RUN("vector", "scalar_tensor_tensor", scrH[:, 0:n], bsq, 1.0 / D,
                      scrH[:, 0:n], OP.mult, OP.subtract, inc="ve")
            W8("scalar", "ve", v_t)
            RUN("scalar", "activation", scrH[:, 0:n], scrH[:, 0:n], AF.Ln,
                bias=epsA[:])
            se_i = RUN("scalar", "activation", istd2b[:, c0:c1], scrH[:, 0:n],
                       AF.Exp, scale=nhA[:], inc="se")
            W8("vector", "se", se_i)
            se_ln2 = se_i
        W8("vector", "se", se_ln2)
        vm2 = {}
        for a0, a1, tag in ((1, 514, "A"), (514, W, "B")):
            m0, m1 = max(a0, 2), a1
            for ci in range(NCT):
                RUN("vector", "tensor_tensor", B3h[:, ci, a0:a1], R2z[:, ci, a0:a1],
                    m2b[:, a0:a1], OP.subtract)
                RUN("vector", "tensor_tensor", B3h[:, ci, a0:a1], B3h[:, ci, a0:a1],
                    istd2b[:, a0:a1], OP.mult)
                RUN("vector", "tensor_scalar", B3h[:, ci, a0:a1], B3h[:, ci, a0:a1],
                    cvec("g2", ci), cvec("b2", ci), OP.mult, OP.add)
                if tag == "A":
                    RUN("vector", "tensor_scalar", B3h[:, ci, 1:2], B3h[:, ci, 1:2],
                        cvec("hmask"), None, OP.mult)
            for ci in range(NCT):
                RUN("vector", "tensor_tensor", B4h[:, ci, m0:m1], B3h[:, ci, m0:m1],
                    B3h[:, ci, m0 - 1:m1 - 1], OP.subtract)
            for ci in range(NCT):
                RUN("vector", "scalar_tensor_tensor", R1a[:, ci, m0:m1],
                    B4h[:, ci, m0:m1], cvec("cmu_r", ci), B3h[:, ci, m0:m1],
                    OP.mult, OP.add)
            vm2[("xr2", tag)] = GINC("vector", "ve")
            for ci in range(NCT):
                RUN("vector", "scalar_tensor_tensor", R1acc[:, ci, m0:m1],
                    B4h[:, ci, m0:m1], cvec("cmu_k", ci), B3h[:, ci, m0:m1],
                    OP.mult, OP.add)
            vm2[("xk2", tag)] = GINC("vector", "ve")
        v_h2 = vm2[("xk2", "B")]
        v_xr2A = vm2[("xr2", "A")]
        v_xr2B = vm2[("xr2", "B")]
        v_xk2A = vm2[("xk2", "A")]
        v_xk2B = vm2[("xk2", "B")]
        v_xr2 = v_xr2B
        v_xk2 = v_xk2B

        def mm_matrix2(stat_base, mov3, banks, first_waits=(), dslab=False,
                       waitsB=()):
            nk = NHT if dslab else NCT
            for wsem, wval in first_waits:
                W8("tensor", wsem, wval)
            tes = []
            for phase in range(2):
                if phase == 1:
                    for wsem, wval in waitsB:
                        W8("tensor", wsem, wval)
                c0, c1, bo = mm_col[phase]
                for ci in range(nk):
                    st = stat_base[:, ci * 128:(ci + 1) * 128]
                    if dslab:
                        mov = mov3(ci, c0, c1)
                    else:
                        mov = mov3[:, ci, c0:c1]
                    RUN("tensor", "matmul", banks[bo][:, 0:c1 - c0], st, mov,
                        start=(ci == 0), stop=(ci == nk - 1),
                        skip_group_check=True)
                tes.append(GINC("tensor", "te"))
            return tes[0], tes[1]

        CUT(5)
        r2_free = {}
        for oj in range(16):
            base_q, qsq, q = load_slab(wr2[oj])
            par = oj % 2
            banks = (pb[0], pb[1]) if par == 0 else (pb[2], pb[3])
            fw = [(qsq, q), ("ve", v_xr2A)]
            if oj >= 2:
                fw.append(("se", r2_free[oj - 2]))
            te_gA, te_gB = mm_matrix2(base_q, R1a, banks, fw,
                                      waitsB=[("ve", v_xr2B)])
            slab_consumed()
            if oj == 0:
                W8("scalar", "ve", v_xk2B)
            for c in range(2):
                c0, c1, _ = mm_col[c]
                W8("scalar", "te", (te_gA, te_gB)[c])
                RUN("scalar", "activation", B3h[:, oj, c0:c1], banks[c][:],
                    AF.Sigmoid)
            r2_free[oj] = GINC("scalar", "se")
        k2_free = {}
        k2_bank_free = {}
        prev_k2mul = [None, None]
        te_wk2_last = None
        for oj in range(16):
            base_q, qsq, q = load_slab(wk2[oj])
            par = oj % 2
            banks = (pb[0], pb[1]) if par == 0 else (pb[2], pb[3])
            fw = [(qsq, q), ("ve", v_xk2A)]
            if oj == 0:
                fw.append(("se", r2_free[15]))
            if oj >= 2:
                fw.append(("se", k2_bank_free[oj - 2]))
            te_gA, te_gB = mm_matrix2(base_q, R1acc, banks, fw,
                                      waitsB=[("ve", v_xk2B)])
            slab_consumed()
            te_wk2_last = te_gB
            s_g = [None, None]
            for c in range(2):
                W8("scalar", "te", (te_gA, te_gB)[c])
                if prev_k2mul[c] is not None:
                    W8("scalar", "ve", prev_k2mul[c])
                s_g[c] = RUN("scalar", "activation", (scrGA, scrGB)[c][:],
                             banks[c][:], AF.Relu, inc="se")
            k2_bank_free[oj] = s_g[1]
            for c in range(2):
                c0, c1, _ = mm_col[c]
                W8("vector", "se", s_g[c])
                prev_k2mul[c] = RUN("vector", "tensor_tensor", B4h[:, oj, c0:c1],
                                    (scrGA, scrGB)[c][:], (scrGA, scrGB)[c][:],
                                    OP.mult, inc="ve")
            k2_free[oj] = prev_k2mul[1]

        W8("sync", "te", te_wk2_last)
        qs_x = {}
        for k in range(8):
            qs_x["qs%d" % (k % 2)] = DMA("sync", "qs%d" % (k % 2), R1x[:, k, :],
                                         xs[2 + 128 * k:2 + 128 * (k + 1), :])

        CUT(6)
        W8("gpsimd", "ve", k2_free[15])
        RUN("gpsimd", "memset", identfB[:], 0.0)
        g_identB = RUN("gpsimd", "affine_select", inc="ge", out=identfB[:],
                       in_=identfB[:], compare_op=OP.not_equal, fill=1.0, base=0,
                       pattern=[[-1, 128]], channel_multiplier=1)

        v2_free = {}
        xadd_free = {}
        for oj in range(16):
            base_q, qsq, q = load_slab(wv2[oj])
            par = oj % 2
            banks = (pb[0], pb[1]) if par == 0 else (pb[2], pb[3])
            fw = [(qsq, q), ("ve", k2_free[15])]
            if oj >= 2:
                fw.append(("ve", v2_free[oj - 2]))
            te_gA, te_gB = mm_matrix2(base_q, B4h, banks, fw)
            slab_consumed()
            for c in range(2):
                c0, c1, _ = mm_col[c]
                W8("vector", "te", (te_gA, te_gB)[c])
                RUN("vector", "tensor_tensor", scrH[:], B3h[:, oj, c0:c1],
                    banks[c][:], OP.mult)
                RUN("vector", "tensor_tensor", R2z[:, oj, c0:c1], R2z[:, oj, c0:c1],
                    scrH[:], OP.add)
            v2_free[oj] = GINC("vector", "ve")
            ci = oj
            W8("tensor", "ve", v2_free[oj])
            if oj == 0:
                W8("tensor", "ge", g_identB)
                for _qn, _qv in qs_x.items():
                    W8("tensor", _qn, _qv)
            else:
                W8("tensor", "ve", xadd_free[oj - 1])
            for kk in range(8):
                bank = pb[4] if kk < 4 else pb[5]
                RUN("tensor", "transpose", bank[:, (kk % 4) * 128:(kk % 4 + 1) * 128],
                    R1x[:, kk, ci * 128:(ci + 1) * 128], identfB[:])
            te_tp = GINC("tensor", "te")
            W8("vector", "te", te_tp)
            for half in range(2):
                RUN("vector", "tensor_tensor",
                    R2z[:, ci, 2 + 512 * half:2 + 512 * (half + 1)],
                    R2z[:, ci, 2 + 512 * half:2 + 512 * (half + 1)],
                    pb[4 + half][:], OP.add)
            xadd_free[oj] = GINC("vector", "ve")
        v_x1 = xadd_free[15]

        CUT(7)
        W8("vector", "ve", v_x1)
        RUN("vector", "memset", onesb[:], 1.0)
        for ci in range(NCT):
            RUN("vector", "tensor_scalar", B3h[:, ci, 2:W], R2z[:, ci, 2:W], 1.0,
                None, OP.mult)
        for ci in range(NCT):
            RUN("vector", "tensor_tensor", B4h[:, ci, 2:W], B3h[:, ci, 2:W],
                B3h[:, ci, 2:W], OP.mult)
        v_x1sq = GINC("vector", "ve")
        W8("tensor", "ve", v_x1sq)
        ffnbanks = []
        for idx, (c0, c1) in enumerate(CH):
            bsum = pb[idx]
            bsq = pb[2 + idx]
            for ci in range(NCT):
                RUN("tensor", "matmul", bsum[:], onesb[:], B3h[:, ci, c0:c1],
                    start=(ci == 0), stop=(ci == NCT - 1), skip_group_check=True)
            for ci in range(NCT):
                RUN("tensor", "matmul", bsq[:], onesb[:], B4h[:, ci, c0:c1],
                    start=(ci == 0), stop=(ci == NCT - 1), skip_group_check=True)
            ffnbanks.append((bsum, bsq))
        te_ffnln = GINC("tensor", "te")
        W8("vector", "te", te_ffnln)
        W8("scalar", "te", te_ffnln)
        for idx, (c0, c1) in enumerate(CH):
            bsum, bsq = ffnbanks[idx]
            RUN("vector", "tensor_scalar", m2b[:, c0:c1], bsum[:], 1.0 / D, None,
                OP.mult)
            RUN("vector", "tensor_tensor", scrH[:], m2b[:, c0:c1], m2b[:, c0:c1],
                OP.mult)
            v_t = RUN("vector", "scalar_tensor_tensor", scrH[:], bsq[:], 1.0 / D,
                      scrH[:], OP.mult, OP.subtract, inc="ve")
            W8("scalar", "ve", v_t)
            RUN("scalar", "activation", scrH[:], scrH[:], AF.Ln, bias=epsA[:])
            se_i = RUN("scalar", "activation", istd2b[:, c0:c1], scrH[:], AF.Exp,
                       scale=nhA[:], inc="se")
            W8("vector", "se", se_i)
            se_ffnln = se_i
        W8("vector", "se", se_ffnln)
        vxn = {}
        for a0, a1, tag in ((2, 514, "A"), (514, W, "B")):
            for ci in range(NCT):
                RUN("vector", "tensor_tensor", B3h[:, ci, a0:a1], R2z[:, ci, a0:a1],
                    m2b[:, a0:a1], OP.subtract)
                RUN("vector", "tensor_tensor", B3h[:, ci, a0:a1], B3h[:, ci, a0:a1],
                    istd2b[:, a0:a1], OP.mult)
            vxn[tag] = GINC("vector", "ve")
        v_xnA = vxn["A"]
        v_xnB = vxn["B"]
        v_xn = v_xnB

        CUT(8)
        def mslice(hj, c0, c1):
            if hj < 32:
                return R1m[:, hj, c0:c1]
            return B4m[:, hj - 32, c0:c1]

        gu_free = {}
        prev_f5 = [None, None]
        te_gu_last = None
        ffn_state["first"] = {0: slab_state["consumed"][-2],
                              1: slab_state["consumed"][-1],
                              2: te_ffnln, 3: te_ffnln}
        ffn_issue(0, wg[0])
        ffn_issue(1, wu[0])
        ffn_issue(2, wg[1])
        ffn_issue(3, wu[1])
        for hj in range(NHT):
            base_g, ge_g = ffn_finish(2 * hj)
            base_u, ge_u = ffn_finish(2 * hj + 1)
            fw = [("ve", ge_g), ("ve", v_xnA)]
            if hj == 0:
                fw.append(("ve", v_x1))
            if hj > 0:
                fw.append(("se", gu_free[("g", hj - 1)]))
            te_gmmA, te_gmmB = mm_matrix2(base_g, B3h, (pb[0], pb[1]), fw,
                                          waitsB=[("ve", v_xnB)])
            ffn_state["consumed"][2 * hj] = cnt["te"]
            fw = [("ve", ge_u)]
            if hj > 0:
                fw.append(("se", gu_free[("u", hj - 1)]))
            te_ummA, te_ummB = mm_matrix2(base_u, B3h, (pb[2], pb[3]), fw)
            ffn_state["consumed"][2 * hj + 1] = cnt["te"]
            te_gu_last = te_ummB
            if hj + 2 < NHT:
                ffn_issue(2 * (hj + 2), wg[hj + 2])
                ffn_issue(2 * (hj + 2) + 1, wu[hj + 2])
            sg = cvec("s_gate", hj)
            su_ = cvec("s_up", hj)
            s_gs = [None, None]
            s_us = [None, None]
            for c in range(2):
                W8("scalar", "te", (te_gmmA, te_gmmB)[c])
                if prev_f5[c] is not None:
                    W8("scalar", "ve", prev_f5[c])
                s_gs[c] = RUN("scalar", "activation", (scrGA, scrGB)[c][:],
                              pb[c][:], AF.Silu, scale=sg, inc="se")
            for c in range(2):
                W8("scalar", "te", (te_ummA, te_ummB)[c])
                s_us[c] = RUN("scalar", "activation",
                              mslice(hj, 512 * c, 512 * (c + 1)),
                              pb[2 + c][:], AF.Copy, scale=su_, inc="se")
            for c in range(2):
                W8("vector", "se", s_us[c])
                prev_f5[c] = RUN("vector", "tensor_tensor",
                                 mslice(hj, 512 * c, 512 * (c + 1)),
                                 mslice(hj, 512 * c, 512 * (c + 1)),
                                 (scrGA, scrGB)[c][:], OP.mult, inc="ve")
            gu_free[("g", hj)] = s_gs[1]
            gu_free[("u", hj)] = s_us[1]
            gu_free[("m", hj)] = prev_f5[1]

        v_m_done = gu_free[("m", NHT - 1)]

        CUT(9)
        dn_free = {}
        load_dslab(wd[0], first_wait=te_gu_last)
        load_dslab(wd[1], first_wait=te_gu_last)
        for oj in range(16):
            base_d, ge_d = dslab_finish(oj)
            par = oj % 2
            banks = (pb[0], pb[1]) if par == 0 else (pb[2], pb[3])
            fw = [("ve", ge_d), ("ve", v_m_done)]
            if oj >= 2:
                fw.append(("ve", dn_free[oj - 2]))
            te_dA, te_dB = mm_matrix2(base_d,
                              lambda ci, c0, c1: mslice(ci, c0 - 2, c1 - 2),
                              banks, fw, dslab=True)
            dslab_state["consumed"].append(cnt["te"])
            if oj + 2 < 16:
                load_dslab(wd[oj + 2])
            sd = cvec("s_down", oj)
            for c in range(2):
                c0, c1, _ = mm_col[c]
                W8("vector", "te", (te_dA, te_dB)[c])
                RUN("vector", "scalar_tensor_tensor", R2z[:, oj, c0:c1],
                    banks[c][:], sd, R2z[:, oj, c0:c1], OP.mult, OP.add)
            dn_free[oj] = GINC("vector", "ve")

        ost_free = [None, None]
        v_ost_prev = None
        for k in range(8):
            buf = k % 2
            if k == 0:
                W8("tensor", "ve", dn_free[14])
                W8("tensor", "ve", dn_free[15])
            else:
                W8("tensor", "ve", v_ost_prev)
            if ost_free[buf] is not None:
                W8("tensor", "qs%d" % buf, ost_free[buf])
            for ci in range(NCT):
                bank = pb[ci // 4]
                RUN("tensor", "transpose", bank[:, (ci % 4) * 128:(ci % 4 + 1) * 128],
                    R2z[:, ci, 2 + 128 * k:2 + 128 * (k + 1)], identfB[:])
            te_tp = GINC("tensor", "te")
            W8("vector", "te", te_tp)
            for q in range(4):
                RUN("vector", "tensor_copy",
                    ostage[buf].rearrange("p (c t) -> p c t", c=16)[:, 4 * q:4 * (q + 1), :],
                    pb[q][:].rearrange("p (c t) -> p c t", c=4))
            v_ost = GINC("vector", "ve")
            v_ost_prev = v_ost
            W8("sync", "ve", v_ost)
            ost_free[buf] = DMA("sync", "qs%d" % buf,
                                y[128 * k:128 * (k + 1), :], ostage[buf])
        W8("sync", "qs0", cnt["qs0"])
        W8("sync", "qs1", cnt["qs1"])

    except _CutDone:
        for eng in ("tensor", "vector", "scalar", "gpsimd"):
            pass
        W8("sync", "te", cnt["te"])
        W8("sync", "ve", cnt["ve"])
        W8("sync", "se", cnt["se"])
        DMA("sync", "qs0", dbg[:, 0:NCT * W], R1.ap())
        DMA("sync", "qs1", dbg[:, NCT * W:NCT * W + NCT * W // 2],
            B3.ap().bitcast(F32))
        for k in range(8):
            DMA("sync", "qs%d" % (k % 2), y[128 * k:128 * (k + 1), :],
                R2.ap()[:, 2048 * k:2048 * (k + 1)])
        W8("sync", "qs0", cnt["qs0"])
        W8("sync", "qs1", cnt["qs1"])
    with nc.Block() as block:
        @block.sync
        def _(e):
            for f in prog["sync"]:
                f(e)

        @block.gpsimd
        def _(e):
            for f in prog["gpsimd"]:
                f(e)

        @block.tensor
        def _(e):
            for f in prog["tensor"]:
                f(e)

        @block.vector
        def _(e):
            for f in prog["vector"]:
                f(e)

        @block.scalar
        def _(e):
            for f in prog["scalar"]:
                f(e)

    es.close()
    return nc


def _swizzle(w, np=np):
    O_, C_ = w.shape
    a = w.reshape(O_ // 128, 128, C_ // 128, 128)
    a = a.transpose(0, 3, 2, 1)
    return np.ascontiguousarray(a.reshape(O_ // 128, 128, C_))


def _vec_tile(v, np=np):
    n = v.shape[0] // 128
    return v.reshape(n, 128).T


def kernel(**inp):
    x = np.asarray(inp["L0_attn_x"], dtype=np.float32)
    nc = _build()

    w_shared = {
        "wk": _swizzle(np.asarray(inp["L0_attn_tm_Wk"], np.float32)),
        "wv": _swizzle(np.asarray(inp["L0_attn_tm_Wv"], np.float32)),
        "wr": _swizzle(np.asarray(inp["L0_attn_tm_Wr"], np.float32)),
        "wo": _swizzle(np.asarray(inp["L0_attn_tm_Wout"], np.float32)),
        "wr2": _swizzle(np.asarray(inp["L0_attn_cm_Wr"], np.float32)),
        "wk2": _swizzle(np.asarray(inp["L0_attn_cm_Wk"], np.float32)),
        "wv2": _swizzle(np.asarray(inp["L0_attn_cm_Wv"], np.float32)),
        "wg": _swizzle(np.asarray(inp["L0_ffn_wq_gate"], np.int32)),
        "wu": _swizzle(np.asarray(inp["L0_ffn_wq_up"], np.int32)),
        "wd": _swizzle(np.asarray(inp["L0_ffn_wq_down"], np.int32)),
    }
    cbase = np.zeros((128, NCON), np.float32)
    cbase[:, CC["eps"][0]] = EPS
    cbase[:, CC["nh"][0]] = -0.5
    cbase[:, CC["weps"][0]] = WKV_EPS
    for nm, key in (("mu_r", "L0_attn_tm_mu_r"), ("mu_k", "L0_attn_tm_mu_k"),
                    ("mu_v", "L0_attn_tm_mu_v"), ("cmu_r", "L0_attn_cm_mu_r"),
                    ("cmu_k", "L0_attn_cm_mu_k"), ("g1", "L0_attn_ln1_g"),
                    ("b1", "L0_attn_ln1_b"), ("g2", "L0_attn_ln2_g"),
                    ("b2", "L0_attn_ln2_b"), ("u", "L0_attn_tm_u"),
                    ("s_down", "L0_ffn_s_down"), ("s_gate", "L0_ffn_s_gate"),
                    ("s_up", "L0_ffn_s_up")):
        a, b = CC[nm]
        cbase[:, a:b] = _vec_tile(np.asarray(inp[key], np.float32))

    in_maps = []
    for c in range(8):
        b, h = c // 2, c % 2
        if h == 0:
            sl = np.concatenate([np.zeros((2, D), np.float32), x[b, 0:TPC]], 0)
        else:
            sl = x[b, TPC - 2:2 * TPC]
        ct = cbase.copy()
        ct[:, CC["hmask"][0]] = 0.0 if h == 0 else 1.0
        m = {"xs": np.ascontiguousarray(sl), "consts": ct}
        m.update(w_shared)
        in_maps.append(m)

    res = run_bass_kernel_spmd(nc, in_maps, core_ids=list(range(8)))
    out = np.empty((4, 2 * TPC, D), np.float32)
    for c in range(8):
        b, h = c // 2, c % 2
        out[b, h * TPC:(h + 1) * TPC] = np.asarray(res.results[c]["y"],
                                                   dtype=np.float32)
    return out
